# version 47
# baseline (speedup 1.0000x reference)
import numpy as np
import ml_dtypes
from contextlib import ExitStack

import concourse.bass as bass
import concourse.mybir as mybir
from concourse.bass_utils import run_bass_kernel_spmd

F32 = mybir.dt.float32
BF16 = mybir.dt.bfloat16
AF = mybir.ActivationFunctionType
ALU = mybir.AluOpType
bfdt = ml_dtypes.bfloat16

C = 1024
NPAIR = 8
CC = 8
EXP_OFF = -12.0
LN_EPS = 1e-5

ENGINES = ("pe", "ac", "dv", "po", "sy")


class Sched:

    def __init__(self):
        self.ops = {e: [] for e in ENGINES}
        self.cnt = {}

    def emit(self, eng, fn, waits=(), inc=None, amt=1):
        self.ops[eng].append((tuple(waits), fn, inc, amt))
        if inc is not None:
            self.cnt[inc] = self.cnt.get(inc, 0) + amt
            return self.cnt[inc]
        return None

    def val(self, sem):
        return self.cnt.get(sem, 0)

    def run(self, eng_name, engine, sems):
        floors = {}
        for waits, fn, inc, amt in self.ops[eng_name]:
            for s, v in waits:
                if v and v > floors.get(s, 0):
                    engine.wait_ge(sems[s], v)
                    floors[s] = v
            instr = fn(engine)
            if inc is not None:
                instr.then_inc(sems[inc], amt)


def build(n_seq=2, n_tt=16):
    T = n_tt * 128
    GS = min(8, n_tt)
    NG = n_tt // GS
    NTG = T // 512
    assert NTG >= 1
    nc = bass.Bass("TRN2", target_bir_lowering=False)

    x_in = nc.dram_tensor("x_in", [n_seq, T, C], F32, kind="ExternalInput")
    w_attn = nc.dram_tensor("w_attn", [C, 3 * C], F32, kind="ExternalInput")
    b_attn = nc.dram_tensor("b_attn", [3 * C], F32, kind="ExternalInput")
    w_proj = nc.dram_tensor("w_proj", [C, C], F32, kind="ExternalInput")
    b_proj = nc.dram_tensor("b_proj", [C], F32, kind="ExternalInput")
    g1r = nc.dram_tensor("g1r", [1, 32, 4, 32], F32, kind="ExternalInput")
    g2r = nc.dram_tensor("g2r", [4, 16, 1, 16], F32, kind="ExternalInput")
    g1i = nc.dram_tensor("g1i", [1, 32, 4, 32], F32, kind="ExternalInput")
    g2i = nc.dram_tensor("g2i", [4, 16, 1, 16], F32, kind="ExternalInput")
    b_real = nc.dram_tensor("b_real", [512], F32, kind="ExternalInput")
    b_imag = nc.dram_tensor("b_imag", [512], F32, kind="ExternalInput")
    mask01 = nc.dram_tensor("mask01", [128, 128], BF16, kind="ExternalInput")
    ident = nc.dram_tensor("ident", [128, 128], F32, kind="ExternalInput")
    out = nc.dram_tensor("out", [n_seq, T, C], F32, kind="ExternalOutput")

    ystage_d = nc.dram_tensor("ystage_d", [n_seq, NPAIR, T, 128], BF16)
    battn_d = nc.dram_tensor("battn_d", [16, 128], BF16)
    w2r_d = nc.dram_tensor("w2r_d", [512, 512], BF16)
    w2i_d = nc.dram_tensor("w2i_d", [512, 512], BF16)

    st = ExitStack()

    def sb(name, shape, dtype):
        return st.enter_context(nc.sbuf_tensor(name, shape, dtype))

    def psum(name, shape):
        return st.enter_context(nc.psum_tensor(name, shape, F32))

    W_sb = sb("W_sb", [128, CC * 3072], BF16)
    Wp_sb = sb("Wp_sb", [128, CC * 1024], BF16)
    Wbig_sb = sb("Wbig_sb", [128, CC * 1024], BF16)
    mask_sb = sb("mask_sb", [128, 128], BF16)
    bqk = sb("bqk", [128, 16], F32)
    bv_bc = sb("bv_bc", [128, 1024], BF16)
    bp_row = sb("bp_row", [1, 1024], BF16)
    bf_row = sb("bf_row", [1, 1024], BF16)
    ones_bf = sb("ones_bf", [1, 128], BF16)
    consts = sb("consts", [128, 4], F32)
    rwave = sb("rwave", [128, 8], F32)
    NLS = 4
    mv_sb = sb("mv_sb", [128, 4 * NLS], F32)
    stats_sb = sb("stats_sb", [128, 12 * NLS], F32)

    bqk_bf = sb("bqk_bf", [128, 16], BF16)
    idf_sb = sb("idf_sb", [128, 128], F32)
    idb_sb = sb("idb_sb", [128, 128], BF16)
    xstage = sb("xstage", [128, 3 * 1024], F32)
    row_bf = sb("row_bf", [1, 1024], BF16)
    wts_bf = sb("wts_bf", [128, 512], BF16)
    p_diag = [sb(f"pd{i}", [128, 256], BF16) for i in range(2)]
    st_w = ExitStack()
    wstage = st_w.enter_context(nc.sbuf_tensor("wstage", [128, 4 * 3072], F32))
    g1_sb = st_w.enter_context(nc.sbuf_tensor("g1_sb", [4, 2048], F32))
    g2_sb = st_w.enter_context(nc.sbuf_tensor("g2_sb", [4, 512], F32))
    st_w.enter_context(nc.sbuf_tensor("w2pad", [128, 8832], F32))
    w2big = st_w.enter_context(nc.sbuf_tensor("w2big", [128, 2 * 2048], BF16))

    ps_big = psum("ps_big", [128, 2048])
    ps_s = [ps_big[:, 0:1024], ps_big[:, 1024:2048]]
    ps_y = [psum("ps_y0", [128, 512]), psum("ps_y1", [128, 512]),
            psum("ps_y2", [128, 512])]
    ps_g = psum("ps_g", [128, 512])

    sem_names = ["pe", "ac", "dv", "po",
                 "q_w", "q_wa", "q_b", "q_wt", "q_g", "q_w2", "q_w3", "q_xc",
                 "q_xs0", "q_xs1", "q_xb0", "q_xb1", "q_xf",
                 "q_ys0", "q_ys1", "q_yf",
                 "q_x1s0", "q_x1s1", "q_x1f", "q_xt", "q_out", "q_m", "q_bt",
                 "q_wn"]
    sems = {n: st.enter_context(nc.semaphore(n)) for n in sem_names}

    S = Sched()

    def emit_qm(fn, waits=()):
        w = [("q_m", S.val("q_m"))] + list(waits)
        return S.emit("ac", fn, waits=w, inc="q_m", amt=16)
    G = {"ps_s0": ("dv", 0), "ps_s1": ("dv", 0),
         "ps_y0": ("dv", 0), "ps_y1": ("dv", 0), "ps_y2": ("dv", 0),
         "ps_g": ("dv", 0),
         "p0": ("pe", 0), "p1": ("pe", 0), "p2": ("pe", 0),
         "qk0": ("pe", 0), "qk1": ("pe", 0),
         "yst0": ("q_ys0", 0), "yst1": ("q_ys1", 0),
         "v_all": ("pe", 0), "x_feat": ("pe", 0), "yfeat": ("pe", 0),
         "x_tok": ("q_out", 0),
         "tmp0": ("dv", 0), "tmp1": ("dv", 0),
         "x1s0": ("q_x1s0", 0), "x1s1": ("q_x1s0", 0),
         "rwave": ("dv", 0), "w2_sb": ("q_m", 0), "g12": ("pe", 0),
         "pd0": ("pe", 0), "pd1": ("pe", 0)}
    ln_guard = [{"ac": 0, "dv": 0, "po": 0} for _ in range(4)]

    S.emit("dv", lambda v: v.memset(consts[:, 0:1], EXP_OFF))
    S.emit("dv", lambda v: v.memset(consts[:, 1:2], LN_EPS))
    S.emit("dv", lambda v: v.memset(consts[:, 2:3], 0.0), inc="dv")
    v_consts = S.val("dv")
    S.emit("dv", lambda v: v.memset(ones_bf[:], 1.0), inc="dv")
    v_ones = S.val("dv")

    def bias_bcast(dst, wait_qb):
        vdv_last = 0
        vpe_last = 0
        for jh in range(2):
            def mm(t, jh=jh, wait_qb=wait_qb):
                return t.matmul(ps_g[:], ones_bf[0:1, :],
                                row_bf[0:1, jh * 512:(jh + 1) * 512],
                                start=True, stop=True)
            vpe_last = S.emit("pe", mm,
                              waits=[("q_b", wait_qb), ("dv", v_ones), G["ps_g"]],
                              inc="pe")
            def cp(v, jh=jh, dst=dst):
                return v.tensor_copy(dst[:, jh * 512:(jh + 1) * 512], ps_g[:])
            vdv_last = S.emit("dv", cp, waits=[("pe", vpe_last)], inc="dv")
            G["ps_g"] = ("dv", vdv_last)
        return vdv_last, vpe_last

    v_qb = S.emit("po", lambda g: g.dma_start(out=row_bf[0:1, :],
                                              in_=b_attn[2048:3072]),
                  inc="q_b", amt=16)
    v_bvbc, v_pe_bv = bias_bcast(bv_bc, v_qb)
    S.emit("po", lambda g: g.dma_start(out=bp_row[0:1, :], in_=b_proj[:]),
           waits=[("q_b", S.val("q_b"))], inc="q_b", amt=16)
    S.emit("po", lambda g: g.dma_start(out=bf_row[0:1, 0:512], in_=b_real[:]),
           waits=[("q_b", S.val("q_b"))], inc="q_b", amt=16)
    v_bfr = S.emit("po", lambda g: g.dma_start(out=bf_row[0:1, 512:1024],
                                               in_=b_imag[:]),
                   waits=[("q_b", S.val("q_b"))], inc="q_b", amt=16)
    v_brows = v_bfr

    for fi, (gd1, gd2) in enumerate([(g1r, g2r), (g1i, g2i)]):
        S.emit("sy", lambda s, gd1=gd1, fi=fi: s.dma_start(
            out=g1_sb[:, fi * 1024:(fi + 1) * 1024],
            in_=gd1[0].rearrange("b c d -> c b d")), inc="q_g", amt=16)
        S.emit("sy", lambda s, gd2=gd2, fi=fi: s.dma_start(
            out=g2_sb[:, fi * 256:(fi + 1) * 256],
            in_=gd2.rearrange("c e one g -> c (e one g)")), inc="q_g", amt=16)
    v_g = S.val("q_g")


    for fi, wdst in enumerate([w2r_d, w2i_d]):
        vpe = 0
        for m in range(8):
            def mm(t, m=m, fi=fi):
                return t.matmul(ps_big[:, m * 256:(m + 1) * 256],
                                g1_sb[:, fi * 1024 + m * 128:fi * 1024 + (m + 1) * 128],
                                g2_sb[:, fi * 256:(fi + 1) * 256],
                                start=True, stop=True)
            w_list = [("q_g", v_g)]
            if m == 0:
                w_list += [G["ps_s0"], G["ps_s1"]]
                if fi == 1:
                    w_list.append(("dv", v_w2cp))
            vpe = S.emit("pe", mm, waits=w_list, inc="pe" if m == 7 else None)
        def cp(v, fi=fi):
            src_ = bass.AP(ps_big, 0, [[2048, 128], [256, 8], [1, 16], [16, 16]])
            return v.tensor_copy(w2big[:, fi * 2048:(fi + 1) * 2048], src_)
        v_w2cp = S.emit("dv", cp, waits=[("pe", vpe)], inc="dv")
        G["ps_s0"] = ("dv", v_w2cp)
        G["ps_s1"] = ("dv", v_w2cp)
        for m in range(8):
            sem = "q_w2" if m % 2 == 0 else "q_w3"
            ring = "sy"
            def stf(g, m=m, fi=fi, wdst=wdst):
                o = wdst.rearrange("(d g) (b e) -> b d g e", g=16, e=16)[4 * m:4 * (m + 1)]
                return g.dma_start(out=o, in_=w2big[:, fi * 2048 + m * 256:
                                                    fi * 2048 + (m + 1) * 256])
            S.emit(ring, stf, waits=[("dv", v_w2cp), (sem, S.val(sem))],
                   inc=sem, amt=16)
    v_w2done = S.val("q_w2")
    v_w3done = S.val("q_w3")

    prev = ([], 0, 0)
    for wave in range(2):
        w_extra = []
        if wave == 1:
            w_extra = [("ac", S.val("ac")), ("dv", S.val("dv"))]
        for i in range(4):
            cc = wave * 4 + i
            S.emit("ac", lambda a, cc=cc, i=i: a.dma_start(
                out=wstage[:, i * 3072:(i + 1) * 3072],
                in_=w_attn[cc * 128:(cc + 1) * 128, :]),
                waits=w_extra, inc="q_wa", amt=16)
        v_wa_ld = S.val("q_wa")
        for i in range(4):
            cc = wave * 4 + i
            if i % 2 == 0:
                S.emit("ac", lambda a, cc=cc, i=i: a.copy(
                    W_sb[:, cc * 3072:(cc + 1) * 3072],
                    wstage[:, i * 3072:(i + 1) * 3072]),
                    waits=[("q_wa", v_wa_ld)], inc="ac")
            else:
                S.emit("dv", lambda v, cc=cc, i=i: v.tensor_copy(
                    W_sb[:, cc * 3072:(cc + 1) * 3072],
                    wstage[:, i * 3072:(i + 1) * 3072]),
                    waits=[("q_wa", v_wa_ld)], inc="dv")
    v_wa_ac = S.val("ac")
    v_wa_dv = S.val("dv")



    S.emit("po", lambda g: g.dma_start(
        out=battn_d[:], in_=b_attn[0:2048].rearrange("(a b) -> a b", b=128)),
        inc="q_bt", amt=16)
    v_battn = S.val("q_bt")



    v_mask_ld = emit_qm(lambda s: s.dma_start(out=mask_sb[:], in_=mask01[:]))
    v_id = emit_qm(lambda s: s.dma_start(out=idf_sb[:], in_=ident[:]))
    v_idb = S.emit("dv", lambda v: v.tensor_copy(idb_sb[:], idf_sb[:]),
                   waits=[("q_m", v_id)], inc="dv")
    v_bqkbf = emit_qm(lambda s: s.dma_start(out=bqk_bf[:], in_=battn_d[:],
                                            transpose=True),
                      waits=[("q_bt", v_battn)])


    v_bqk = S.emit("ac", lambda a: a.copy(bqk[:], bqk_bf[:]),
                   waits=[("q_m", v_bqkbf)], inc="ac")


    wbig_state = {"guard": ("ac", 0)}

    def emit_wbig_chunk(cc2):
        r0 = (cc2 % 4) * 128
        wq = [("q_w2", v_w2done), ("q_w3", v_w3done)]
        if cc2 < 4:
            S.emit("po", lambda g, r0=r0, cc2=cc2: g.dma_start(
                out=Wbig_sb[:, cc2 * 1024:cc2 * 1024 + 512],
                in_=w2r_d[r0:r0 + 128, :]), waits=wq, inc="q_wt", amt=16)
            S.emit("po", lambda g, r0=r0, cc2=cc2: g.dma_start(
                out=Wbig_sb[:, cc2 * 1024 + 512:(cc2 + 1) * 1024],
                in_=w2i_d[r0:r0 + 128, :]), inc="q_wt", amt=16)
        else:
            v_wt = S.emit("po", lambda g, r0=r0: g.dma_start(
                out=wts_bf[:, 0:512], in_=w2i_d[r0:r0 + 128, :]),
                waits=wq + [wbig_state["guard"]], inc="q_wn", amt=16)
            S.emit("po", lambda g, r0=r0, cc2=cc2: g.dma_start(
                out=Wbig_sb[:, cc2 * 1024 + 512:(cc2 + 1) * 1024],
                in_=w2r_d[r0:r0 + 128, :]), inc="q_wt", amt=16)
            vneg = S.emit("ac", lambda a, cc2=cc2: a.activation(
                Wbig_sb[:, cc2 * 1024:cc2 * 1024 + 512],
                wts_bf[:, 0:512], AF.Identity,
                bias=consts[:, 2:3], scale=-1.0),
                waits=[("q_wn", v_wt), ("dv", v_consts)], inc="ac")
            wbig_state["guard"] = ("ac", vneg)
        if cc2 == CC - 1:
            v_wbig_holder["v"] = [("ac", S.val("ac")),
                                  ("q_wt", S.val("q_wt"))]

    v_wbig_holder = {}
    for dc in range(CC):
        S.emit("po", lambda g, dc=dc: g.dma_start(
            out=Wp_sb[:, dc * 1024:(dc + 1) * 1024],
            in_=w_proj[dc * 128:(dc + 1) * 128, :], max_dma_last_dim=8192),
            inc="q_w", amt=16)
    v_w = S.val("q_w")
    st_w.close()
    v_wa_dv = max(v_wa_dv, S.val("dv"))
    v_setup_pe = S.val("pe")

    for s_i in range(n_seq):
        st_seq = ExitStack()
        x_feat = st_seq.enter_context(
            nc.sbuf_tensor(f"x_feat_{s_i}", [128, CC * T], BF16))
        st_c = ExitStack()

        def sbc(name, shape, dtype):
            return st_c.enter_context(nc.sbuf_tensor(name, shape, dtype))

        v_all = sbc(f"v_all_{s_i}", [128, n_tt * 1040], BF16)
        qp = [sbc(f"qp{i}_{s_i}", [128, T], BF16) for i in range(2)]
        kp = [sbc(f"kp{i}_{s_i}", [128, T], BF16) for i in range(2)]
        p_sb = [sbc(f"p{i}_{s_i}", [128, 2048], BF16) for i in range(3)]
        ystage = [sbc(f"yst{i}_{s_i}", [128, n_tt * 128], BF16) for i in range(2)]
        S.emit("dv", lambda v, v_all=v_all: v.memset(
            bass.AP(v_all, 64, [[n_tt * 1040, 128], [65, n_tt * 16]]), 1.0),
            waits=[G["x_tok"], G["v_all"], ("ac", v_wa_ac), ("dv", v_wa_dv),
                   ("pe", v_setup_pe)],
            inc="dv")
        v_vones = S.val("dv")
        xslot_sem = ["q_xc", "q_xs0", "q_xs1"]
        tbank = [ps_y[0], ps_y[1], ps_y[2], ps_g]
        xf_tile = []
        tp_vals = []
        v_ld = []

        def emit_xload(tt):
            sl = tt % 3
            wl = []
            if tt >= 3:
                wl.append(("pe", tp_vals[tt - 3]))
            elif s_i > 0:
                wl.append(("pe", prev_tp[tt - 3]))
            S.emit("po", lambda g, s_i=s_i, tt=tt, sl=sl: g.dma_start(
                out=xstage[:, sl * 1024:(sl + 1) * 1024],
                in_=x_in[s_i][tt * 128:(tt + 1) * 128, :]),
                waits=wl, inc=xslot_sem[sl], amt=16)
            v_ld.append(S.val(xslot_sem[sl]))

        def emit_xtp(tt):
            sl = tt % 3
            for c in range(CC):
                bank = tbank[(2 * tt + c // 4) % 4]
                q = c % 4
                def tp(t, bank=bank, q=q, sl=sl, c=c):
                    return t.matmul(bank[:, q * 128:(q + 1) * 128],
                                    xstage[:, sl * 1024 + c * 128:
                                           sl * 1024 + (c + 1) * 128],
                                    idf_sb[:], is_transpose=True,
                                    start=(q == 0), stop=(q == 3))
                wl = []
                if c in (0, 4):
                    wl = [(xslot_sem[sl], v_ld[tt]), ("q_m", v_id)]
                    if tt >= 2:
                        wl.append(("dv", xf_tile[tt - 2]))
                    else:
                        b = (2 * tt + c // 4) % 4
                        wl.append(G["ps_y%d" % b] if b < 3 else G["ps_g"])
                r = S.emit("pe", tp, waits=wl,
                           inc="pe" if c == CC - 1 else None)
            tp_vals.append(r)

        def emit_xevac(tt):
            for half in range(2):
                bank = tbank[(2 * tt + half) % 4]
                def ev(v, tt=tt, half=half, bank=bank, x_feat=x_feat):
                    o = bass.AP(x_feat, tt * 128 + half * 4 * T,
                                [[CC * T, 128], [T, 4], [1, 128]])
                    return v.tensor_copy(o, bank[:])
                r = S.emit("dv", ev,
                           waits=[("pe", tp_vals[tt]), G["x_feat"],
                                  ("ac", v_wa_ac), ("dv", v_wa_dv),
                                  ("pe", v_setup_pe)],
                           inc="dv" if half == 1 else None)
            xf_tile.append(r)

        def emit_vmm(tt):
            reg = ps_s[tt % 2]
            vpe = 0
            for cc in range(CC):
                for jh in range(2):
                    def mm(t, tt=tt, cc=cc, jh=jh, reg=reg, x_feat=x_feat):
                        return t.matmul(
                            reg[:, jh * 512:(jh + 1) * 512],
                            x_feat[:, cc * T + tt * 128:cc * T + (tt + 1) * 128],
                            W_sb[:, cc * 3072 + 2048 + jh * 512:
                                 cc * 3072 + 2048 + (jh + 1) * 512],
                            start=(cc == 0), stop=(cc == CC - 1))
                    w_list = [("dv", xf_tile[tt]),
                              ("ac", v_wa_ac), ("dv", v_wa_dv)]
                    if cc == 0 and jh == 0:
                        w_list.append(G["ps_s%d" % (tt % 2)])
                    r = S.emit("pe", mm, waits=w_list,
                               inc="pe" if (cc == CC - 1 and jh == 1) else None)
                    if r:
                        vpe = r
            def ev(v, tt=tt, half=tt % 2, v_all=v_all):
                o = bass.AP(v_all, tt * 1040, [[n_tt * 1040, 128], [65, 16], [1, 64]])
                i0 = bass.AP(ps_big, half * 1024, [[2048, 128], [64, 16], [1, 64]])
                i1 = bass.AP(bv_bc, 0, [[1024, 128], [64, 16], [1, 64]])
                return v.tensor_tensor(o, i0, i1, ALU.add)
            vdv = S.emit("dv", ev,
                         waits=[("pe", vpe),
                                ("dv", max(v_bvbc, v_vones)),
                                G["v_all"]],
                         inc="dv")
            G["ps_s%d" % (tt % 2)] = ("dv", vdv)
            return vdv

        for tt in range(min(3, n_tt)):
            emit_xload(tt)
        for tt in range(min(4, n_tt)):
            emit_xtp(tt)
            emit_xevac(tt)
            if tt + 3 < n_tt:
                emit_xload(tt + 3)
        v_vall_ready = 0
        for tt in range(n_tt):
            v_vall_ready = emit_vmm(tt)
            if tt + 4 < n_tt:
                emit_xtp(tt + 4)
                emit_xevac(tt + 4)
            if tt + 7 < n_tt:
                emit_xload(tt + 7)
        prev_tp = tp_vals
        G["ps_y0"] = ("dv", xf_tile[n_tt - 2])
        G["ps_y1"] = ("dv", xf_tile[n_tt - 2])
        G["ps_y2"] = ("dv", xf_tile[n_tt - 1])
        G["ps_g"] = ("dv", xf_tile[n_tt - 1])

        def make_qk_jobs(hp, alt_bank=False):
            jobs = []
            par = hp % 2
            ji = [0]
            for tg in range(NTG):
                for which in range(2):
                    j0 = which * 1024 + 128 * hp
                    dstbuf = qp[par] if which == 0 else kp[par]
                    bcol = which * 8 + hp
                    bname = ("ps_y0" if (alt_bank and (2 * tg + which) % 2)
                             else "ps_g")
                    bank = ps_y[0] if bname == "ps_y0" else ps_g
                    def job(which=which, j0=j0, dstbuf=dstbuf, bcol=bcol,
                            tg=tg, par=par, bname=bname, bank=bank):
                        vpe = 0
                        for cc in range(CC):
                            def mm(t, cc=cc, j0=j0, tg=tg, x_feat=x_feat,
                                   bank=bank):
                                return t.matmul(
                                    bank[:],
                                    W_sb[:, cc * 3072 + j0:cc * 3072 + j0 + 128],
                                    x_feat[:, cc * T + tg * 512:cc * T + (tg + 1) * 512],
                                    start=(cc == 0), stop=(cc == CC - 1))
                            w_list = [("dv", xf_tile[4 * tg + 3]),
                                      ("ac", v_wa_ac), ("dv", v_wa_dv)]
                            if cc == 0:
                                w_list.append(G[bname])
                            r = S.emit("pe", mm, waits=w_list,
                                       inc="pe" if cc == CC - 1 else None)
                            if r:
                                vpe = r
                        def evq(v, dstbuf=dstbuf, tg=tg, bcol=bcol,
                                bank=bank):
                            return v.tensor_scalar(
                                dstbuf[:, tg * 512:(tg + 1) * 512], bank[:],
                                bqk[:, bcol:bcol + 1], None, ALU.add)
                        vdv = S.emit("dv", evq,
                                     waits=[("pe", vpe), ("ac", v_bqk),
                                            G["qk%d" % par]], inc="dv")
                        G[bname] = ("dv", vdv)
                        return (tg if which == 1 else None, vdv)
                    jobs.append(job)
            return jobs

        tg_need = [((g + 1) * GS * 128) // 512 - 1 for g in range(NG)]

        def consume(job_ret, vq):
            if job_ret is None:
                return
            tg_fin, vdv = job_ret
            if tg_fin is None:
                return
            for g in range(NG):
                if tg_need[g] == tg_fin:
                    vq[g] = vdv

        qk_done = {}
        ys_val = []
        v_ev67 = {}
        vq0 = [0] * NG
        for j in make_qk_jobs(0, alt_bank=True):
            consume(j(), vq0)
        qk_done[0] = vq0

        for hp in range(NPAIR):
            par = hp % 2
            vq_ready = qk_done[hp]
            pend = make_qk_jobs(hp + 1) if hp + 1 < NPAIR else []
            vq_next = [0] * NG
            qk_done[hp + 1] = vq_next
            job_i = 0
            kslot = [0]
            spar = 0
            for g_ in range(NG):
                base = 128 * GS * g_
                qbase = GS * g_
                kk_hi = qbase + GS
                nslots = 2 * GS
                first_of_bank = {}
                exp_of_kk = {}
                exp_chunks = {}
                mask_of_kk = {}

                def emit_scores(kk):
                    nonlocal spar
                    tail_start = max(base, 128 * kk)
                    tail_end = base + 128 * GS
                    pos = tail_start
                    while pos < tail_end:
                        ln = min(512, tail_end - pos)
                        half = spar
                        for e in range(2):
                            def mm(t, e=e, kk=kk, pos=pos, ln=ln, half=half,
                                   kpb=kp[par], qpb=qp[par]):
                                off = half * 1024 + e * 512
                                return t.matmul(
                                    ps_big[:, off:off + ln],
                                    kpb[e * 64:(e + 1) * 64,
                                        kk * 128:(kk + 1) * 128],
                                    qpb[e * 64:(e + 1) * 64, pos:pos + ln],
                                    start=True, stop=True)
                            w_list = []
                            if e == 0:
                                w_list = [("dv", vq_ready[g_]),
                                          G["ps_s%d" % half]]
                            r = S.emit("pe", mm, waits=w_list,
                                       inc="pe" if e == 1 else None)
                            if r:
                                vpe = r
                        def ex(a, kk=kk, pos=pos, ln=ln, half=half, base=base,
                               psel=p_sb[kk % 3]):
                            i = bass.AP(ps_big, half * 1024,
                                        [[2048, 128], [512, 2], [1, ln]])
                            o = bass.AP(psel, pos - base,
                                        [[2048, 128], [1024, 2], [1, ln]])
                            return a.activation(o, i, AF.Exp, bias=consts[:, 0:1],
                                                scale=0.125)
                        vac = S.emit("ac", ex,
                                     waits=[("pe", vpe), ("dv", v_consts),
                                            G["p%d" % (kk % 3)]], inc="ac")
                        G["ps_s%d" % half] = ("ac", vac)
                        exp_of_kk[kk] = vac
                        exp_chunks.setdefault(kk, []).append(vac)
                        spar ^= 1
                        pos += ln
                    if kk >= qbase:
                        def mk(v, kk=kk, base=base, psel=p_sb[kk % 3],
                               pd=p_diag[kk % 2]):
                            pp = bass.AP(psel, 128 * kk - base,
                                         [[2048, 128], [1024, 2], [1, 128]])
                            po_ = bass.AP(pd, 0, [[256, 128], [128, 2], [1, 128]])
                            mm_ = bass.AP(mask_sb, 0, [[128, 128], [0, 2], [1, 128]])
                            return v.tensor_tensor(po_, pp, mm_, ALU.mult)
                        mask_of_kk[kk] = S.emit(
                            "dv", mk, waits=[("ac", exp_of_kk[kk]),
                                             ("q_m", v_mask_ld),
                                             G["pd%d" % (kk % 2)]], inc="dv")

                def emit_pv(kk):
                    nonlocal job_i
                    qql_lo = max(0, kk - qbase)
                    pv_list = [(e, qql) for e in range(2)
                               for qql in range(qql_lo, GS)
                               if not (kk >= qbase and qql == qql_lo)]
                    nondiag_last = pv_list[-1] if pv_list else None
                    if kk >= qbase:
                        pv_list += [(0, qql_lo), (1, qql_lo)]
                    vpe_nd = 0
                    vpe_pv = 0
                    for e, qql in pv_list:
                        s3 = 2 * qql + e
                        b_ = min(s3 // 7, 2)
                        off = (s3 - 7 * b_) * 65
                        first = not first_of_bank.get(b_, False)
                        first_of_bank[b_] = True
                        hi_abs = nslots - 1 if b_ == 2 else                             min(7 * (b_ + 1), nslots) - 1
                        is_bank_last = (s3 == hi_abs) and (kk == qbase + qql)
                        is_diag = (kk >= qbase and qql == qql_lo)
                        if is_diag:
                            lhs_ap = bass.AP(p_diag[kk % 2], e * 128,
                                             [[256, 128], [1, 128]])
                        else:
                            lhs_ap = p_sb[kk % 3][:, e * 1024 + qql * 128:
                                                  e * 1024 + (qql + 1) * 128]
                        def mm(t, e=e, kk=kk, b_=b_, off=off, hp=hp,
                               first=first, is_bank_last=is_bank_last,
                               lhs_ap=lhs_ap, v_all=v_all):
                            vcol = kk * 1040 + (2 * hp + e) * 65
                            return t.matmul(
                                ps_y[b_][:, off:off + 65],
                                lhs_ap,
                                v_all[:, vcol:vcol + 65],
                                start=first, stop=is_bank_last)
                        ts_ = max(base, 128 * kk)
                        ch = min((base + qql * 128 - ts_) // 512,
                                 len(exp_chunks[kk]) - 1)
                        w_list = [("ac", exp_chunks[kk][ch])]
                        if is_diag:
                            w_list.append(("dv", mask_of_kk[kk]))
                        if first:
                            w_list.append(G["ps_y%d" % b_])
                            w_list.append(("dv", v_vall_ready))
                        is_last_emitted = (e, qql) == pv_list[-1]
                        is_nd_last = (e, qql) == nondiag_last
                        r = S.emit("pe", mm, waits=w_list,
                                   inc="pe" if (is_last_emitted or is_nd_last)
                                   else None)
                        if r and is_nd_last:
                            vpe_nd = r
                        if r:
                            vpe_pv = r
                    G["p%d" % (kk % 3)] = ("pe",
                                           vpe_nd if nondiag_last else vpe_pv)
                    if kk >= qbase:
                        G["pd%d" % (kk % 2)] = ("pe", vpe_pv)
                    for b_ in range(3):
                        lo = 7 * b_
                        hi = min(7 * (b_ + 1), nslots) - 1
                        if lo >= nslots:
                            continue
                        fin_kk = qbase + hi // 2
                        if kk == fin_kk:
                            nsl = hi - lo + 1
                            def rc(v, b_=b_, nsl=nsl):
                                src = bass.AP(ps_y[b_], 64, [[512, 128], [65, nsl]])
                                return v.reciprocal(rwave[:, 0:nsl], src)
                            vr1 = S.emit("dv", rc,
                                         waits=[("pe", vpe_pv), G["rwave"]],
                                         inc="dv")
                            def tt_(v, b_=b_, nsl=nsl, lo=lo, base=base,
                                    yst_=ystage[par]):
                                o = bass.AP(yst_, base + 64 * lo,
                                            [[n_tt * 128, 128], [64, nsl], [1, 64]])
                                i0 = bass.AP(ps_y[b_], 0,
                                             [[512, 128], [65, nsl], [1, 64]])
                                i1 = bass.AP(rwave, 0, [[8, 128], [1, nsl], [0, 64]])
                                return v.tensor_tensor(o, i0, i1, ALU.mult)
                            vdv = S.emit("dv", tt_,
                                         waits=[("dv", vr1), G["yst%d" % par],
                                                ("q_w2", v_w2done),
                                                ("q_w3", v_w3done)],
                                         inc="dv")
                            G["ps_y%d" % b_] = ("dv", vdv)
                            G["rwave"] = ("dv", vdv)
                    kslot[0] += 1
                    if job_i < len(pend):
                        consume(pend[job_i](), vq_next)
                        job_i += 1

                for kk in range(kk_hi):
                    emit_scores(kk)
                    if kk > 0:
                        emit_pv(kk - 1)
                emit_pv(kk_hi - 1)
            while job_i < len(pend):
                consume(pend[job_i](), vq_next)
                job_i += 1
            G["qk%d" % par] = ("pe", S.val("pe"))
            if s_i == 0:
                emit_wbig_chunk(hp)
            v_evacs = S.val("dv")
            def yst_store(s, s_i=s_i, hp=hp, yst_=ystage[par]):
                src = bass.AP(yst_, 0, [[n_tt * 128, 128], [128, n_tt], [1, 128]])
                dst = ystage_d[s_i, hp].rearrange("(tt p) d -> p tt d", p=128)
                return s.dma_start(out=dst, in_=src)
            if hp < 6:
                vys = S.emit("sy", yst_store, waits=[("dv", v_evacs)],
                             inc="q_ys%d" % par, amt=16)
                G["yst%d" % par] = ("q_ys%d" % par, vys)
                ys_val.append(vys)
            else:
                v_ev67[hp] = v_evacs
            if hp == 6:
                v_xf_free = S.val("pe")
                for hp2 in range(6):
                    S.emit("sy", lambda a, s_i=s_i, hp2=hp2, x_feat=x_feat:
                           a.dma_start(
                               out=x_feat[:, hp2 * T:(hp2 + 1) * T],
                               in_=ystage_d[s_i, hp2][:, :],
                               transpose=True),
                           waits=[("q_ys%d" % (hp2 % 2), ys_val[hp2]),
                                  ("pe", v_xf_free)],
                           inc="q_yf", amt=16)
        nbg = (n_tt + 7) // 8
        y67_banks = {6: [ps_y[0], ps_y[1]], 7: [ps_y[2], ps_g]}
        y67_names = {6: ["ps_y0", "ps_y1"], 7: ["ps_y2", "ps_g"]}
        v_y67 = 0
        for hp2 in (6, 7):
            par2 = hp2 % 2
            vtp = 0
            for gb in range(nbg):
                ntl = min(8, n_tt - gb * 8)
                bank = y67_banks[hp2][gb].bitcast(BF16)
                bname = y67_names[hp2][gb]
                for c in range(ntl):
                    def ytp(t, bank=bank, c=c, gb=gb, ntl=ntl,
                            yst_=ystage[par2]):
                        return t.matmul(
                            bank[:, c * 128:(c + 1) * 128],
                            yst_[:, (gb * 8 + c) * 128:
                                 (gb * 8 + c + 1) * 128],
                            idb_sb[:], is_transpose=True,
                            start=(c == 0), stop=(c == ntl - 1))
                    wl = []
                    if c == 0:
                        wl = [("dv", max(v_ev67[hp2], v_idb)), G[bname]]
                    vtp = S.emit("pe", ytp, waits=wl,
                                 inc="pe" if c == ntl - 1 else None)
                def yev(v, hp2=hp2, gb=gb, bank=bank, ntl=ntl, x_feat=x_feat):
                    return v.tensor_copy(
                        x_feat[:, hp2 * T + gb * 1024:
                               hp2 * T + gb * 1024 + ntl * 128],
                        bank[:, 0:ntl * 128])
                v_y67 = S.emit("dv", yev, waits=[("pe", vtp)], inc="dv")
                G[bname] = ("dv", v_y67)
            G["yst%d" % par2] = ("pe", vtp)
        G["v_all"] = ("pe", S.val("pe"))
        v_attn_pe = S.val("pe")
        st_c.close()
        st_b = ExitStack()
        x_tok = st_b.enter_context(nc.sbuf_tensor(f"x_tok_{s_i}", [128, n_tt * 1024], F32))
        x1stage = st_b.enter_context(nc.sbuf_tensor(f"x1stage_{s_i}", [128, 2 * 1024], BF16))
        yfeat = x_feat
        v_yf = S.val("q_yf")

        xt_q4 = []
        ntq = max(1, n_tt // 4)
        for q4 in range(n_tt // ntq):
            def xt(s, s_i=s_i, q4=q4, ntq=ntq, x_tok=x_tok):
                src = x_in[s_i].rearrange("(tt p) c -> p tt c", p=128)[
                    :, q4 * ntq:(q4 + 1) * ntq]
                dst = x_tok[:].rearrange("p (tt c) -> p tt c", c=1024)[
                    :, q4 * ntq:(q4 + 1) * ntq]
                return s.dma_start(out=dst, in_=src)
            S.emit("po", xt, waits=[G["x_tok"], ("pe", v_attn_pe),
                                    ("q_ys0", S.val("q_ys0")),
                                    ("q_ys1", S.val("q_ys1")),
                                    ("q_w2", v_w2done), ("q_w3", v_w3done),
                                    ("q_xt", S.val("q_xt"))],
                   inc="q_xt", amt=16)
            xt_q4.append(S.val("q_xt"))
        v_xt = S.val("q_xt")

        def linear_ln(win_sb, w_wait, feat_ready_fn, bias_bc, v_bias, x1_mode,
                      v_po_base=0, yfeat=yfeat, x_tok=x_tok):
            stageB = [None] * n_tt

            def stage_a(tt):
                feat_ready = feat_ready_fn(tt)
                reg = ps_s[tt % 2]
                vpe = 0
                for jh in range(2):
                    def mmb(t, tt=tt, jh=jh, reg=reg, bias_row=bias_bc):
                        return t.matmul(
                            reg[:, jh * 512:(jh + 1) * 512],
                            ones_bf[0:1, :],
                            bias_row[0:1, jh * 512:(jh + 1) * 512],
                            start=True, stop=False)
                    w_list = [("q_b", v_brows), ("dv", v_ones)]
                    if jh == 0:
                        w_list.append(G["ps_s%d" % (tt % 2)])
                    S.emit("pe", mmb, waits=w_list)
                for dc in range(CC):
                    for jh in range(2):
                        def mm(t, tt=tt, dc=dc, jh=jh, reg=reg, win_sb=win_sb,
                               yfeat=yfeat):
                            return t.matmul(
                                reg[:, jh * 512:(jh + 1) * 512],
                                yfeat[:, dc * T + tt * 128:dc * T + (tt + 1) * 128],
                                win_sb[:, dc * 1024 + jh * 512:dc * 1024 + (jh + 1) * 512],
                                start=False, stop=(dc == CC - 1))
                        w_list = [feat_ready] + list(w_wait)
                        r = S.emit("pe", mm, waits=w_list,
                                   inc="pe" if (dc == CC - 1 and jh == 1) else None)
                        if r:
                            vpe = r
                par = tt % 2
                sl = tt % NLS
                mv = mv_sb[:, sl * 4:sl * 4 + 4]
                stt = stats_sb[:, sl * 12:sl * 12 + 12]
                def t2(v, tt=tt, par=par, x_tok=x_tok, reg=reg):
                    xt_ = x_tok[:, tt * 1024:(tt + 1) * 1024]
                    return v.tensor_tensor(xt_, xt_, reg, ALU.add)
                v_t2 = S.emit("dv", t2, waits=[("q_xt", xt_q4[tt // ntq]),
                                               ("pe", vpe),
                                               ("po", v_po_base)],
                              inc="dv")
                G["ps_s%d" % par] = ("dv", v_t2)
                def st1(v, tt=tt, x_tok=x_tok, stt=stt):
                    return v.bn_stats(stt[:, 0:6],
                                      x_tok[:, tt * 1024:tt * 1024 + 512])
                S.emit("dv", st1, waits=[("dv", v_t2)], inc="dv")
                def st2(v, tt=tt, x_tok=x_tok, stt=stt):
                    return v.bn_stats(stt[:, 6:12],
                                      x_tok[:, tt * 1024 + 512:(tt + 1) * 1024])
                vs2 = S.emit("dv", st2, inc="dv")
                def ag(v, mv=mv, stt=stt):
                    return v.bn_aggr(mv[:, 0:2], stt[:])
                vag = S.emit("dv", ag,
                             waits=[("dv", vs2), ("ac", ln_guard[sl]["ac"]),
                                    ("dv", ln_guard[sl]["dv"]),
                                    ("po", ln_guard[sl].get("po", 0))], inc="dv")
                def l1(a, mv=mv):
                    return a.activation(mv[:, 2:3], mv[:, 1:2], AF.Ln,
                                        bias=consts[:, 1:2], scale=1.0)
                va1 = S.emit("ac", l1, waits=[("dv", vag)], inc="ac")
                def l2(a, mv=mv):
                    return a.activation(mv[:, 2:3], mv[:, 2:3], AF.Exp,
                                        bias=consts[:, 2:3], scale=-0.5)
                va2 = S.emit("ac", l2, waits=[("ac", va1)], inc="ac")

                def stage_b1(tt=tt, par=par, sl=sl, mv=mv, va2=va2):
                    def nm(v, mv=mv):
                        return v.tensor_scalar(mv[:, 3:4], mv[:, 0:1],
                                               mv[:, 2:3], -1.0, ALU.mult, ALU.mult)
                    vnm = S.emit("dv", nm, waits=[("ac", va2)], inc="dv")
                    if x1_mode:
                        def xb(a, tt=tt, par=par, x_tok=x_tok, mv=mv):
                            return a.activation(
                                x1stage[:, par * 1024:(par + 1) * 1024],
                                x_tok[:, tt * 1024:(tt + 1) * 1024],
                                AF.Identity, bias=mv[:, 3:4], scale=mv[:, 2:3])
                        vxb = S.emit("ac", xb,
                                     waits=[("dv", vnm), G["x1s%d" % par]],
                                     inc="ac")
                        x1xb[tt] = vxb
                        def xf32(g, tt=tt, x_tok=x_tok, mv=mv):
                            return g.tensor_scalar(x_tok[:, tt * 1024:(tt + 1) * 1024],
                                                   x_tok[:, tt * 1024:(tt + 1) * 1024],
                                                   mv[:, 2:3], mv[:, 3:4],
                                                   ALU.mult, ALU.add)
                        vz = S.emit("po", xf32, waits=[("ac", vxb)], inc="po")
                        ln_guard[sl] = {"ac": vxb, "dv": 0, "po": vz}
                    else:
                        def xf32(g, tt=tt, x_tok=x_tok, mv=mv):
                            return g.tensor_scalar(x_tok[:, tt * 1024:(tt + 1) * 1024],
                                                   x_tok[:, tt * 1024:(tt + 1) * 1024],
                                                   mv[:, 2:3], mv[:, 3:4],
                                                   ALU.mult, ALU.add)
                        vz = S.emit("po", xf32, waits=[("dv", vnm)], inc="po")
                        def so(s, s_i=s_i, tt=tt, x_tok=x_tok):
                            return s.dma_start(out=out[s_i, tt * 128:(tt + 1) * 128, :],
                                               in_=x_tok[:, tt * 1024:(tt + 1) * 1024])
                        S.emit("sy", so, waits=[("po", vz)], inc="q_out", amt=16)
                        ln_guard[sl] = {"ac": va2, "dv": 0, "po": vz}
                return stage_b1

            def stage_b2(tt):
                par = tt % 2
                bank = tbank[tt % 4].bitcast(BF16)
                for c in range(CC):
                    def x1tp(t, bank=bank, c=c, par=par):
                        return t.matmul(
                            bank[:, c * 128:(c + 1) * 128],
                            x1stage[:, par * 1024 + c * 128:
                                    par * 1024 + (c + 1) * 128],
                            idb_sb[:], is_transpose=True,
                            start=(c == 0), stop=(c == CC - 1))
                    wl = []
                    if c == 0:
                        wl = [("ac", x1xb[tt]), ("dv", v_idb)]
                        if tt >= 4:
                            wl.append(("ac", x1evac[tt - 4]))
                        else:
                            b = tt % 4
                            wl.append(G["ps_y%d" % b] if b < 3 else G["ps_g"])
                    vtp = S.emit("pe", x1tp, waits=wl,
                                 inc="pe" if c == CC - 1 else None)
                G["x1s%d" % par] = ("pe", vtp)
                def x1ev(a, tt=tt, bank=bank, yfeat=yfeat):
                    o = bass.AP(yfeat, tt * 128,
                                [[CC * T, 128], [T, 8], [1, 128]])
                    return a.copy(o, bank[:, 0:1024])
                x1evac[tt] = S.emit("ac", x1ev, waits=[("pe", vtp)], inc="ac")

            for tt in range(n_tt):
                if x1_mode and tt > 2:
                    stage_b2(tt - 3)
                if tt > 0:
                    stageB[tt - 1]()
                stageB[tt] = stage_a(tt)
            if x1_mode and n_tt >= 3:
                stage_b2(n_tt - 3)
            stageB[n_tt - 1]()
            if x1_mode:
                for tt in range(max(0, n_tt - 2), n_tt):
                    stage_b2(tt)

        x1evac = {}
        x1xb = {}
        linear_ln(Wp_sb, [("q_w", v_w), ("dv", v_y67)],
                  lambda tt: ("q_yf", v_yf),
                  bp_row, 0, True)
        for b in range(4):
            last = max(t for t in x1evac if t % 4 == b)
            gk = "ps_y%d" % b if b < 3 else "ps_g"
            G[gk] = ("ac", x1evac[last])
        linear_ln(Wbig_sb, v_wbig_holder["v"],
                  lambda tt: ("ac", x1evac[tt]),
                  bf_row, 0, False, v_po_base=S.val("po"))
        G["yfeat"] = ("pe", S.val("pe"))
        G["x_feat"] = ("pe", S.val("pe"))
        G["x_tok"] = ("q_out", S.val("q_out"))
        st_b.close()
        st_seq.close()

    v_out_final = S.val("q_out")

    with st:
        with nc.Block() as block:
            @block.tensor
            def _(t):
                S.run("pe", t, sems)

            @block.scalar
            def _(a):
                S.run("ac", a, sems)

            @block.vector
            def _(v):
                S.run("dv", v, sems)

            @block.gpsimd
            def _(g):
                S.run("po", g, sems)

            @block.sync
            def _(s):
                S.run("sy", s, sems)
                s.wait_ge(sems["q_out"], v_out_final)

    return nc


_NC_CACHE = {}


def kernel(**inputs):
    x = np.asarray(inputs["L0_attn_x"], dtype=np.float32)
    w_attn = np.asarray(inputs["L0_attn_W_attn"], dtype=np.float32)
    b_attn = np.asarray(inputs["L0_attn_b_attn"], dtype=np.float32)
    w_proj = np.asarray(inputs["L0_attn_W_proj"], dtype=np.float32)
    b_proj = np.asarray(inputs["L0_attn_b_proj"], dtype=np.float32)
    g1r = np.asarray(inputs["L0_ffn_G1_R"], dtype=np.float32)
    g2r = np.asarray(inputs["L0_ffn_G2_R"], dtype=np.float32)
    g1i = np.asarray(inputs["L0_ffn_G1_I"], dtype=np.float32)
    g2i = np.asarray(inputs["L0_ffn_G2_I"], dtype=np.float32)
    b_real = np.asarray(inputs["L0_ffn_bias_real"], dtype=np.float32)
    b_imag = np.asarray(inputs["L0_ffn_bias_imag"], dtype=np.float32)

    mask = np.triu(np.ones((128, 128), dtype=np.float32)).astype(bfdt)
    eye = np.eye(128, dtype=np.float32)

    if "nc" not in _NC_CACHE:
        _NC_CACHE["nc"] = build(2, 16)
    nc = _NC_CACHE["nc"]

    in_maps = []
    for core in range(8):
        in_maps.append({
            "x_in": np.ascontiguousarray(x[core * 2:(core + 1) * 2]),
            "w_attn": w_attn, "b_attn": b_attn,
            "w_proj": w_proj, "b_proj": b_proj,
            "g1r": g1r, "g2r": g2r, "g1i": g1i, "g2i": g2i,
            "b_real": b_real, "b_imag": b_imag,
            "mask01": mask, "ident": eye,
        })
    res = run_bass_kernel_spmd(nc, in_maps, core_ids=list(range(8)))
    return np.concatenate([r["out"] for r in res.results], axis=0)


# version 48
# speedup vs baseline: 1.0720x; 1.0720x over previous
import numpy as np
import ml_dtypes
from contextlib import ExitStack

import concourse.bass as bass
import concourse.mybir as mybir
from concourse.bass_utils import run_bass_kernel_spmd

F32 = mybir.dt.float32
BF16 = mybir.dt.bfloat16
AF = mybir.ActivationFunctionType
ALU = mybir.AluOpType
bfdt = ml_dtypes.bfloat16

C = 1024
NPAIR = 8
CC = 8
EXP_OFF = -12.0
LN_EPS = 1e-5

ENGINES = ("pe", "ac", "dv", "po", "sy")


class Sched:

    def __init__(self):
        self.ops = {e: [] for e in ENGINES}
        self.cnt = {}

    def emit(self, eng, fn, waits=(), inc=None, amt=1):
        self.ops[eng].append((tuple(waits), fn, inc, amt))
        if inc is not None:
            self.cnt[inc] = self.cnt.get(inc, 0) + amt
            return self.cnt[inc]
        return None

    def val(self, sem):
        return self.cnt.get(sem, 0)

    def run(self, eng_name, engine, sems):
        floors = {}
        for waits, fn, inc, amt in self.ops[eng_name]:
            for s, v in waits:
                if v and v > floors.get(s, 0):
                    engine.wait_ge(sems[s], v)
                    floors[s] = v
            instr = fn(engine)
            if inc is not None:
                instr.then_inc(sems[inc], amt)


def build(n_seq=2, n_tt=16):
    T = n_tt * 128
    GS = min(8, n_tt)
    NG = n_tt // GS
    NTG = T // 512
    assert NTG >= 1
    nc = bass.Bass("TRN2", target_bir_lowering=False)

    x_in = nc.dram_tensor("x_in", [n_seq, T, C], F32, kind="ExternalInput")
    w_attn = nc.dram_tensor("w_attn", [C, 3 * C], F32, kind="ExternalInput")
    b_attn = nc.dram_tensor("b_attn", [3 * C], F32, kind="ExternalInput")
    w_proj = nc.dram_tensor("w_proj", [C, C], F32, kind="ExternalInput")
    b_proj = nc.dram_tensor("b_proj", [C], F32, kind="ExternalInput")
    g1r = nc.dram_tensor("g1r", [1, 32, 4, 32], F32, kind="ExternalInput")
    g2r = nc.dram_tensor("g2r", [4, 16, 1, 16], F32, kind="ExternalInput")
    g1i = nc.dram_tensor("g1i", [1, 32, 4, 32], F32, kind="ExternalInput")
    g2i = nc.dram_tensor("g2i", [4, 16, 1, 16], F32, kind="ExternalInput")
    b_real = nc.dram_tensor("b_real", [512], F32, kind="ExternalInput")
    b_imag = nc.dram_tensor("b_imag", [512], F32, kind="ExternalInput")
    mask01 = nc.dram_tensor("mask01", [128, 128], BF16, kind="ExternalInput")
    ident = nc.dram_tensor("ident", [128, 128], F32, kind="ExternalInput")
    out = nc.dram_tensor("out", [n_seq, T, C], F32, kind="ExternalOutput")

    ystage_d = nc.dram_tensor("ystage_d", [n_seq, NPAIR, T, 128], BF16)
    battn_d = nc.dram_tensor("battn_d", [16, 128], BF16)
    w2r_d = nc.dram_tensor("w2r_d", [512, 512], BF16)
    w2i_d = nc.dram_tensor("w2i_d", [512, 512], BF16)

    st = ExitStack()

    def sb(name, shape, dtype):
        return st.enter_context(nc.sbuf_tensor(name, shape, dtype))

    def psum(name, shape):
        return st.enter_context(nc.psum_tensor(name, shape, F32))

    W_sb = sb("W_sb", [128, CC * 3072], BF16)
    Wp_sb = sb("Wp_sb", [128, CC * 1024], BF16)
    Wbig_sb = sb("Wbig_sb", [128, CC * 1024], BF16)
    mask_sb = sb("mask_sb", [128, 128], BF16)
    bqk = sb("bqk", [128, 16], F32)
    bv_bc = sb("bv_bc", [128, 1024], BF16)
    bp_row = sb("bp_row", [1, 1024], BF16)
    bf_row = sb("bf_row", [1, 1024], BF16)
    ones_bf = sb("ones_bf", [1, 128], BF16)
    consts = sb("consts", [128, 4], F32)
    rwave = sb("rwave", [128, 8], F32)
    NLS = 4
    mv_sb = sb("mv_sb", [128, 4 * NLS], F32)
    stats_sb = sb("stats_sb", [128, 12 * NLS], F32)

    bqk_bf = sb("bqk_bf", [128, 16], BF16)
    idf_sb = sb("idf_sb", [128, 128], F32)
    idb_sb = sb("idb_sb", [128, 128], BF16)
    xstage = sb("xstage", [128, 3 * 1024], F32)
    row_bf = sb("row_bf", [1, 1024], BF16)
    wts_bf = sb("wts_bf", [128, 512], BF16)
    p_diag = [sb(f"pd{i}", [128, 256], BF16) for i in range(2)]
    st_w = ExitStack()
    wstage = st_w.enter_context(nc.sbuf_tensor("wstage", [128, 4 * 3072], F32))
    g1_sb = st_w.enter_context(nc.sbuf_tensor("g1_sb", [4, 2048], F32))
    g2_sb = st_w.enter_context(nc.sbuf_tensor("g2_sb", [4, 512], F32))
    st_w.enter_context(nc.sbuf_tensor("w2pad", [128, 8832], F32))
    w2big = st_w.enter_context(nc.sbuf_tensor("w2big", [128, 2 * 2048], BF16))

    ps_big = psum("ps_big", [128, 2048])
    ps_s = [ps_big[:, 0:1024], ps_big[:, 1024:2048]]
    ps_y = [psum("ps_y0", [128, 512]), psum("ps_y1", [128, 512]),
            psum("ps_y2", [128, 512])]
    ps_g = psum("ps_g", [128, 512])

    sem_names = ["pe", "ac", "dv", "po",
                 "q_w", "q_wa", "q_b", "q_wt", "q_g", "q_w2", "q_w3", "q_xc",
                 "q_xs0", "q_xs1", "q_xb0", "q_xb1", "q_xf",
                 "q_ys0", "q_ys1", "q_yf",
                 "q_x1s0", "q_x1s1", "q_x1f", "q_xt", "q_out", "q_m", "q_bt",
                 "q_wn"]
    sems = {n: st.enter_context(nc.semaphore(n)) for n in sem_names}

    S = Sched()

    def emit_qm(fn, waits=()):
        w = [("q_m", S.val("q_m"))] + list(waits)
        return S.emit("ac", fn, waits=w, inc="q_m", amt=16)
    G = {"ps_s0": ("dv", 0), "ps_s1": ("dv", 0),
         "ps_y0": ("dv", 0), "ps_y1": ("dv", 0), "ps_y2": ("dv", 0),
         "ps_g": ("dv", 0),
         "p0": ("pe", 0), "p1": ("pe", 0), "p2": ("pe", 0),
         "qk0": ("pe", 0), "qk1": ("pe", 0),
         "yst0": ("q_ys0", 0), "yst1": ("q_ys1", 0),
         "v_all": ("pe", 0), "x_feat": ("pe", 0), "yfeat": ("pe", 0),
         "x_tok": ("q_out", 0),
         "tmp0": ("dv", 0), "tmp1": ("dv", 0),
         "x1s0": ("q_x1s0", 0), "x1s1": ("q_x1s0", 0),
         "rwave": ("dv", 0), "w2_sb": ("q_m", 0), "g12": ("pe", 0),
         "pd0": ("pe", 0), "pd1": ("pe", 0)}
    ln_guard = [{"ac": 0, "dv": 0, "po": 0} for _ in range(4)]

    S.emit("dv", lambda v: v.memset(consts[:, 0:1], EXP_OFF))
    S.emit("dv", lambda v: v.memset(consts[:, 1:2], LN_EPS))
    S.emit("dv", lambda v: v.memset(consts[:, 2:3], 0.0), inc="dv")
    v_consts = S.val("dv")
    S.emit("dv", lambda v: v.memset(ones_bf[:], 1.0), inc="dv")
    v_ones = S.val("dv")

    def bias_bcast(dst, wait_qb):
        vdv_last = 0
        vpe_last = 0
        for jh in range(2):
            def mm(t, jh=jh, wait_qb=wait_qb):
                return t.matmul(ps_g[:], ones_bf[0:1, :],
                                row_bf[0:1, jh * 512:(jh + 1) * 512],
                                start=True, stop=True)
            vpe_last = S.emit("pe", mm,
                              waits=[("q_b", wait_qb), ("dv", v_ones), G["ps_g"]],
                              inc="pe")
            def cp(v, jh=jh, dst=dst):
                return v.tensor_copy(dst[:, jh * 512:(jh + 1) * 512], ps_g[:])
            vdv_last = S.emit("dv", cp, waits=[("pe", vpe_last)], inc="dv")
            G["ps_g"] = ("dv", vdv_last)
        return vdv_last, vpe_last

    v_qb = S.emit("po", lambda g: g.dma_start(out=row_bf[0:1, :],
                                              in_=b_attn[2048:3072]),
                  inc="q_b", amt=16)
    v_bvbc, v_pe_bv = bias_bcast(bv_bc, v_qb)
    S.emit("po", lambda g: g.dma_start(out=bp_row[0:1, :], in_=b_proj[:]),
           waits=[("q_b", S.val("q_b"))], inc="q_b", amt=16)
    S.emit("po", lambda g: g.dma_start(out=bf_row[0:1, 0:512], in_=b_real[:]),
           waits=[("q_b", S.val("q_b"))], inc="q_b", amt=16)
    v_bfr = S.emit("po", lambda g: g.dma_start(out=bf_row[0:1, 512:1024],
                                               in_=b_imag[:]),
                   waits=[("q_b", S.val("q_b"))], inc="q_b", amt=16)
    v_brows = v_bfr

    for fi, (gd1, gd2) in enumerate([(g1r, g2r), (g1i, g2i)]):
        S.emit("sy", lambda s, gd1=gd1, fi=fi: s.dma_start(
            out=g1_sb[:, fi * 1024:(fi + 1) * 1024],
            in_=gd1[0].rearrange("b c d -> c b d")), inc="q_g", amt=16)
        S.emit("sy", lambda s, gd2=gd2, fi=fi: s.dma_start(
            out=g2_sb[:, fi * 256:(fi + 1) * 256],
            in_=gd2.rearrange("c e one g -> c (e one g)")), inc="q_g", amt=16)
    v_g = S.val("q_g")


    for fi, wdst in enumerate([w2r_d, w2i_d]):
        vpe = 0
        for m in range(8):
            def mm(t, m=m, fi=fi):
                return t.matmul(ps_big[:, m * 256:(m + 1) * 256],
                                g1_sb[:, fi * 1024 + m * 128:fi * 1024 + (m + 1) * 128],
                                g2_sb[:, fi * 256:(fi + 1) * 256],
                                start=True, stop=True)
            w_list = [("q_g", v_g)]
            if m == 0:
                w_list += [G["ps_s0"], G["ps_s1"]]
                if fi == 1:
                    w_list.append(("dv", v_w2cp))
            vpe = S.emit("pe", mm, waits=w_list, inc="pe" if m == 7 else None)
        def cp(v, fi=fi):
            src_ = bass.AP(ps_big, 0, [[2048, 128], [256, 8], [1, 16], [16, 16]])
            return v.tensor_copy(w2big[:, fi * 2048:(fi + 1) * 2048], src_)
        v_w2cp = S.emit("dv", cp, waits=[("pe", vpe)], inc="dv")
        G["ps_s0"] = ("dv", v_w2cp)
        G["ps_s1"] = ("dv", v_w2cp)
        for m in range(8):
            sem = "q_w2" if m % 2 == 0 else "q_w3"
            ring = "sy"
            def stf(g, m=m, fi=fi, wdst=wdst):
                o = wdst.rearrange("(d g) (b e) -> b d g e", g=16, e=16)[4 * m:4 * (m + 1)]
                return g.dma_start(out=o, in_=w2big[:, fi * 2048 + m * 256:
                                                    fi * 2048 + (m + 1) * 256])
            S.emit(ring, stf, waits=[("dv", v_w2cp), (sem, S.val(sem))],
                   inc=sem, amt=16)
    v_w2done = S.val("q_w2")
    v_w3done = S.val("q_w3")

    prev = ([], 0, 0)
    for wave in range(2):
        w_extra = []
        if wave == 1:
            w_extra = [("ac", S.val("ac")), ("dv", S.val("dv"))]
        for i in range(4):
            cc = wave * 4 + i
            S.emit("ac", lambda a, cc=cc, i=i: a.dma_start(
                out=wstage[:, i * 3072:(i + 1) * 3072],
                in_=w_attn[cc * 128:(cc + 1) * 128, :]),
                waits=w_extra, inc="q_wa", amt=16)
        v_wa_ld = S.val("q_wa")
        for i in range(4):
            cc = wave * 4 + i
            if i % 2 == 0:
                S.emit("ac", lambda a, cc=cc, i=i: a.copy(
                    W_sb[:, cc * 3072:(cc + 1) * 3072],
                    wstage[:, i * 3072:(i + 1) * 3072]),
                    waits=[("q_wa", v_wa_ld)], inc="ac")
            else:
                S.emit("dv", lambda v, cc=cc, i=i: v.tensor_copy(
                    W_sb[:, cc * 3072:(cc + 1) * 3072],
                    wstage[:, i * 3072:(i + 1) * 3072]),
                    waits=[("q_wa", v_wa_ld)], inc="dv")
    v_wa_ac = S.val("ac")
    v_wa_dv = S.val("dv")



    S.emit("po", lambda g: g.dma_start(
        out=battn_d[:], in_=b_attn[0:2048].rearrange("(a b) -> a b", b=128)),
        inc="q_bt", amt=16)
    v_battn = S.val("q_bt")



    v_mask_ld = emit_qm(lambda s: s.dma_start(out=mask_sb[:], in_=mask01[:]))
    v_id = emit_qm(lambda s: s.dma_start(out=idf_sb[:], in_=ident[:]))
    v_idb = S.emit("dv", lambda v: v.tensor_copy(idb_sb[:], idf_sb[:]),
                   waits=[("q_m", v_id)], inc="dv")
    v_bqkbf = emit_qm(lambda s: s.dma_start(out=bqk_bf[:], in_=battn_d[:],
                                            transpose=True),
                      waits=[("q_bt", v_battn)])


    v_bqk = S.emit("ac", lambda a: a.copy(bqk[:], bqk_bf[:]),
                   waits=[("q_m", v_bqkbf)], inc="ac")


    wbig_state = {"guard": ("ac", 0)}

    def emit_wbig_chunk(cc2):
        r0 = (cc2 % 4) * 128
        wq = [("q_w2", v_w2done), ("q_w3", v_w3done)]
        if cc2 < 4:
            S.emit("po", lambda g, r0=r0, cc2=cc2: g.dma_start(
                out=Wbig_sb[:, cc2 * 1024:cc2 * 1024 + 512],
                in_=w2r_d[r0:r0 + 128, :]), waits=wq, inc="q_wt", amt=16)
            S.emit("po", lambda g, r0=r0, cc2=cc2: g.dma_start(
                out=Wbig_sb[:, cc2 * 1024 + 512:(cc2 + 1) * 1024],
                in_=w2i_d[r0:r0 + 128, :]), inc="q_wt", amt=16)
        else:
            v_wt = S.emit("po", lambda g, r0=r0: g.dma_start(
                out=wts_bf[:, 0:512], in_=w2i_d[r0:r0 + 128, :]),
                waits=wq + [wbig_state["guard"]], inc="q_wn", amt=16)
            S.emit("po", lambda g, r0=r0, cc2=cc2: g.dma_start(
                out=Wbig_sb[:, cc2 * 1024 + 512:(cc2 + 1) * 1024],
                in_=w2r_d[r0:r0 + 128, :]), inc="q_wt", amt=16)
            vneg = S.emit("ac", lambda a, cc2=cc2: a.activation(
                Wbig_sb[:, cc2 * 1024:cc2 * 1024 + 512],
                wts_bf[:, 0:512], AF.Identity,
                bias=consts[:, 2:3], scale=-1.0),
                waits=[("q_wn", v_wt), ("dv", v_consts)], inc="ac")
            wbig_state["guard"] = ("ac", vneg)
        if cc2 == CC - 1:
            v_wbig_holder["v"] = [("ac", S.val("ac")),
                                  ("q_wt", S.val("q_wt"))]

    v_wbig_holder = {}
    for dc in range(CC):
        S.emit("po", lambda g, dc=dc: g.dma_start(
            out=Wp_sb[:, dc * 1024:(dc + 1) * 1024],
            in_=w_proj[dc * 128:(dc + 1) * 128, :], max_dma_last_dim=8192),
            inc="q_w", amt=16)
    v_w = S.val("q_w")
    st_w.close()
    v_wa_dv = max(v_wa_dv, S.val("dv"))
    v_setup_pe = S.val("pe")

    for s_i in range(n_seq):
        st_seq = ExitStack()
        x_feat = st_seq.enter_context(
            nc.sbuf_tensor(f"x_feat_{s_i}", [128, CC * T], BF16))
        st_c = ExitStack()

        def sbc(name, shape, dtype):
            return st_c.enter_context(nc.sbuf_tensor(name, shape, dtype))

        v_all = sbc(f"v_all_{s_i}", [128, n_tt * 1040], BF16)
        qp = [sbc(f"qp{i}_{s_i}", [128, T], BF16) for i in range(2)]
        kp = [sbc(f"kp{i}_{s_i}", [128, T], BF16) for i in range(2)]
        p_sb = [sbc(f"p{i}_{s_i}", [128, 2048], BF16) for i in range(3)]
        ystage = [sbc(f"yst{i}_{s_i}", [128, n_tt * 128], BF16) for i in range(2)]
        S.emit("dv", lambda v, v_all=v_all: v.memset(
            bass.AP(v_all, 64, [[n_tt * 1040, 128], [65, n_tt * 16]]), 1.0),
            waits=[G["x_tok"], G["v_all"], ("ac", v_wa_ac), ("dv", v_wa_dv),
                   ("pe", v_setup_pe)],
            inc="dv")
        v_vones = S.val("dv")
        xslot_sem = ["q_xc", "q_xs0", "q_xs1"]
        tbank = [ps_y[0], ps_y[1], ps_y[2], ps_g]
        xf_tile = []
        tp_vals = []
        v_ld = []

        def emit_xload(tt):
            sl = tt % 3
            wl = []
            if tt >= 3:
                wl.append(("pe", tp_vals[tt - 3]))
            elif s_i > 0:
                wl.append(("pe", prev_tp[tt - 3]))
            S.emit("po", lambda g, s_i=s_i, tt=tt, sl=sl: g.dma_start(
                out=xstage[:, sl * 1024:(sl + 1) * 1024],
                in_=x_in[s_i][tt * 128:(tt + 1) * 128, :]),
                waits=wl, inc=xslot_sem[sl], amt=16)
            v_ld.append(S.val(xslot_sem[sl]))

        def emit_xtp(tt):
            sl = tt % 3
            for c in range(CC):
                bank = tbank[(2 * tt + c // 4) % 4]
                q = c % 4
                def tp(t, bank=bank, q=q, sl=sl, c=c):
                    return t.matmul(bank[:, q * 128:(q + 1) * 128],
                                    xstage[:, sl * 1024 + c * 128:
                                           sl * 1024 + (c + 1) * 128],
                                    idf_sb[:], is_transpose=True,
                                    start=(q == 0), stop=(q == 3))
                wl = []
                if c in (0, 4):
                    wl = [(xslot_sem[sl], v_ld[tt]), ("q_m", v_id)]
                    if tt >= 2:
                        wl.append(("dv", xf_tile[tt - 2]))
                    else:
                        b = (2 * tt + c // 4) % 4
                        wl.append(G["ps_y%d" % b] if b < 3 else G["ps_g"])
                r = S.emit("pe", tp, waits=wl,
                           inc="pe" if c == CC - 1 else None)
            tp_vals.append(r)

        def emit_xevac(tt):
            for half in range(2):
                bank = tbank[(2 * tt + half) % 4]
                def ev(v, tt=tt, half=half, bank=bank, x_feat=x_feat):
                    o = bass.AP(x_feat, tt * 128 + half * 4 * T,
                                [[CC * T, 128], [T, 4], [1, 128]])
                    return v.tensor_copy(o, bank[:])
                r = S.emit("dv", ev,
                           waits=[("pe", tp_vals[tt]), G["x_feat"],
                                  ("ac", v_wa_ac), ("dv", v_wa_dv),
                                  ("pe", v_setup_pe)],
                           inc="dv" if half == 1 else None)
            xf_tile.append(r)

        def emit_vmm(tt):
            reg = ps_s[tt % 2]
            vpe = 0
            for cc in range(CC):
                for jh in range(2):
                    def mm(t, tt=tt, cc=cc, jh=jh, reg=reg, x_feat=x_feat):
                        return t.matmul(
                            reg[:, jh * 512:(jh + 1) * 512],
                            x_feat[:, cc * T + tt * 128:cc * T + (tt + 1) * 128],
                            W_sb[:, cc * 3072 + 2048 + jh * 512:
                                 cc * 3072 + 2048 + (jh + 1) * 512],
                            start=(cc == 0), stop=(cc == CC - 1))
                    w_list = [("dv", xf_tile[tt]),
                              ("ac", v_wa_ac), ("dv", v_wa_dv)]
                    if cc == 0 and jh == 0:
                        w_list.append(G["ps_s%d" % (tt % 2)])
                    r = S.emit("pe", mm, waits=w_list,
                               inc="pe" if (cc == CC - 1 and jh == 1) else None)
                    if r:
                        vpe = r
            def ev(v, tt=tt, half=tt % 2, v_all=v_all):
                o = bass.AP(v_all, tt * 1040, [[n_tt * 1040, 128], [65, 16], [1, 64]])
                i0 = bass.AP(ps_big, half * 1024, [[2048, 128], [64, 16], [1, 64]])
                i1 = bass.AP(bv_bc, 0, [[1024, 128], [64, 16], [1, 64]])
                return v.tensor_tensor(o, i0, i1, ALU.add)
            vdv = S.emit("dv", ev,
                         waits=[("pe", vpe),
                                ("dv", max(v_bvbc, v_vones)),
                                G["v_all"]],
                         inc="dv")
            G["ps_s%d" % (tt % 2)] = ("dv", vdv)
            return vdv

        for tt in range(min(3, n_tt)):
            emit_xload(tt)
        for tt in range(min(4, n_tt)):
            emit_xtp(tt)
            emit_xevac(tt)
            if tt + 3 < n_tt:
                emit_xload(tt + 3)
        v_vall_ready = 0
        for tt in range(n_tt):
            v_vall_ready = emit_vmm(tt)
            if tt + 4 < n_tt:
                emit_xtp(tt + 4)
                emit_xevac(tt + 4)
            if tt + 7 < n_tt:
                emit_xload(tt + 7)
        prev_tp = tp_vals
        G["ps_y0"] = ("dv", xf_tile[n_tt - 2])
        G["ps_y1"] = ("dv", xf_tile[n_tt - 2])
        G["ps_y2"] = ("dv", xf_tile[n_tt - 1])
        G["ps_g"] = ("dv", xf_tile[n_tt - 1])

        def make_qk_jobs(hp, alt_bank=False):
            jobs = []
            par = hp % 2
            ji = [0]
            for tg in range(NTG):
                for which in range(2):
                    j0 = which * 1024 + 128 * hp
                    dstbuf = qp[par] if which == 0 else kp[par]
                    bcol = which * 8 + hp
                    bname = ("ps_y0" if (alt_bank and (2 * tg + which) % 2
                                         and tg <= tg_need[0])
                             else "ps_g")
                    bank = ps_y[0] if bname == "ps_y0" else ps_g
                    def half(lo, which=which, j0=j0, dstbuf=dstbuf, bcol=bcol,
                             tg=tg, par=par, bname=bname, bank=bank):
                        vpe = 0
                        for cc in range(lo, lo + CC // 2):
                            def mm(t, cc=cc, j0=j0, tg=tg, x_feat=x_feat,
                                   bank=bank):
                                return t.matmul(
                                    bank[:],
                                    W_sb[:, cc * 3072 + j0:cc * 3072 + j0 + 128],
                                    x_feat[:, cc * T + tg * 512:cc * T + (tg + 1) * 512],
                                    start=(cc == 0), stop=(cc == CC - 1))
                            w_list = [("dv", xf_tile[4 * tg + 3]),
                                      ("ac", v_wa_ac), ("dv", v_wa_dv)]
                            if cc == 0:
                                w_list.append(G[bname])
                            r = S.emit("pe", mm, waits=w_list,
                                       inc="pe" if cc == CC - 1 else None)
                            if r:
                                vpe = r
                        if lo > 0:
                            def evq(v, dstbuf=dstbuf, tg=tg, bcol=bcol,
                                    bank=bank):
                                return v.tensor_scalar(
                                    dstbuf[:, tg * 512:(tg + 1) * 512], bank[:],
                                    bqk[:, bcol:bcol + 1], None, ALU.add)
                            vdv = S.emit("dv", evq,
                                         waits=[("pe", vpe), ("ac", v_bqk),
                                                G["qk%d" % par]], inc="dv")
                            G[bname] = ("dv", vdv)
                            return (tg if which == 1 else None, vdv)
                        return None
                    jobs.append(lambda half=half: half(0))
                    jobs.append(lambda half=half: half(CC // 2))
            return jobs

        tg_need = [((g + 1) * GS * 128) // 512 - 1 for g in range(NG)]

        def consume(job_ret, vq):
            if job_ret is None:
                return
            tg_fin, vdv = job_ret
            if tg_fin is None:
                return
            for g in range(NG):
                if tg_need[g] == tg_fin:
                    vq[g] = vdv

        qk_done = {}
        ys_val = []
        v_ev67 = {}
        vq0 = [0] * NG
        jobs0 = make_qk_jobs(0, alt_bank=True)
        n_up = min(len(jobs0), 4 * (tg_need[0] + 1))
        for j in jobs0[:n_up]:
            consume(j(), vq0)
        jobs0_left = jobs0[n_up:]
        qk_done[0] = vq0

        for hp in range(NPAIR):
            par = hp % 2
            vq_ready = qk_done[hp]
            vq_next = [0] * NG
            qk_done[hp + 1] = vq_next
            pend = ([(j, vq_next) for j in make_qk_jobs(hp + 1)]
                    if hp + 1 < NPAIR else [])
            if hp == 0:
                pend = [(j, vq0) for j in jobs0_left] + pend
            job_i = 0
            kslot = [0]
            spar = 0
            for g_ in range(NG):
                base = 128 * GS * g_
                qbase = GS * g_
                kk_hi = qbase + GS
                nslots = 2 * GS
                first_of_bank = {}
                exp_of_kk = {}
                exp_chunks = {}
                mask_of_kk = {}

                def emit_scores(kk):
                    nonlocal spar
                    tail_start = max(base, 128 * kk)
                    tail_end = base + 128 * GS
                    pos = tail_start
                    while pos < tail_end:
                        ln = min(512, tail_end - pos)
                        half = spar
                        for e in range(2):
                            def mm(t, e=e, kk=kk, pos=pos, ln=ln, half=half,
                                   kpb=kp[par], qpb=qp[par]):
                                off = half * 1024 + e * 512
                                return t.matmul(
                                    ps_big[:, off:off + ln],
                                    kpb[e * 64:(e + 1) * 64,
                                        kk * 128:(kk + 1) * 128],
                                    qpb[e * 64:(e + 1) * 64, pos:pos + ln],
                                    start=True, stop=True)
                            w_list = []
                            if e == 0:
                                w_list = [("dv", vq_ready[g_]),
                                          G["ps_s%d" % half]]
                            r = S.emit("pe", mm, waits=w_list,
                                       inc="pe" if e == 1 else None)
                            if r:
                                vpe = r
                        def ex(a, kk=kk, pos=pos, ln=ln, half=half, base=base,
                               psel=p_sb[kk % 3]):
                            i = bass.AP(ps_big, half * 1024,
                                        [[2048, 128], [512, 2], [1, ln]])
                            o = bass.AP(psel, pos - base,
                                        [[2048, 128], [1024, 2], [1, ln]])
                            return a.activation(o, i, AF.Exp, bias=consts[:, 0:1],
                                                scale=0.125)
                        vac = S.emit("ac", ex,
                                     waits=[("pe", vpe), ("dv", v_consts),
                                            G["p%d" % (kk % 3)]], inc="ac")
                        G["ps_s%d" % half] = ("ac", vac)
                        exp_of_kk[kk] = vac
                        exp_chunks.setdefault(kk, []).append(vac)
                        spar ^= 1
                        pos += ln
                    if kk >= qbase:
                        def mk(v, kk=kk, base=base, psel=p_sb[kk % 3],
                               pd=p_diag[kk % 2]):
                            pp = bass.AP(psel, 128 * kk - base,
                                         [[2048, 128], [1024, 2], [1, 128]])
                            po_ = bass.AP(pd, 0, [[256, 128], [128, 2], [1, 128]])
                            mm_ = bass.AP(mask_sb, 0, [[128, 128], [0, 2], [1, 128]])
                            return v.tensor_tensor(po_, pp, mm_, ALU.mult)
                        mask_of_kk[kk] = S.emit(
                            "dv", mk, waits=[("ac", exp_of_kk[kk]),
                                             ("q_m", v_mask_ld),
                                             G["pd%d" % (kk % 2)]], inc="dv")

                def emit_pv(kk):
                    nonlocal job_i
                    qql_lo = max(0, kk - qbase)
                    pv_list = [(e, qql) for e in range(2)
                               for qql in range(qql_lo, GS)
                               if not (kk >= qbase and qql == qql_lo)]
                    nondiag_last = pv_list[-1] if pv_list else None
                    if kk >= qbase:
                        pv_list += [(0, qql_lo), (1, qql_lo)]
                    vpe_nd = 0
                    vpe_pv = 0
                    for e, qql in pv_list:
                        s3 = 2 * qql + e
                        b_ = min(s3 // 7, 2)
                        off = (s3 - 7 * b_) * 65
                        first = not first_of_bank.get(b_, False)
                        first_of_bank[b_] = True
                        hi_abs = nslots - 1 if b_ == 2 else                             min(7 * (b_ + 1), nslots) - 1
                        is_bank_last = (s3 == hi_abs) and (kk == qbase + qql)
                        is_diag = (kk >= qbase and qql == qql_lo)
                        if is_diag:
                            lhs_ap = bass.AP(p_diag[kk % 2], e * 128,
                                             [[256, 128], [1, 128]])
                        else:
                            lhs_ap = p_sb[kk % 3][:, e * 1024 + qql * 128:
                                                  e * 1024 + (qql + 1) * 128]
                        def mm(t, e=e, kk=kk, b_=b_, off=off, hp=hp,
                               first=first, is_bank_last=is_bank_last,
                               lhs_ap=lhs_ap, v_all=v_all):
                            vcol = kk * 1040 + (2 * hp + e) * 65
                            return t.matmul(
                                ps_y[b_][:, off:off + 65],
                                lhs_ap,
                                v_all[:, vcol:vcol + 65],
                                start=first, stop=is_bank_last)
                        ts_ = max(base, 128 * kk)
                        ch = min((base + qql * 128 - ts_) // 512,
                                 len(exp_chunks[kk]) - 1)
                        w_list = [("ac", exp_chunks[kk][ch])]
                        if is_diag:
                            w_list.append(("dv", mask_of_kk[kk]))
                        if first:
                            w_list.append(G["ps_y%d" % b_])
                            w_list.append(("dv", v_vall_ready))
                        is_last_emitted = (e, qql) == pv_list[-1]
                        is_nd_last = (e, qql) == nondiag_last
                        r = S.emit("pe", mm, waits=w_list,
                                   inc="pe" if (is_last_emitted or is_nd_last)
                                   else None)
                        if r and is_nd_last:
                            vpe_nd = r
                        if r:
                            vpe_pv = r
                    G["p%d" % (kk % 3)] = ("pe",
                                           vpe_nd if nondiag_last else vpe_pv)
                    if kk >= qbase:
                        G["pd%d" % (kk % 2)] = ("pe", vpe_pv)
                    for b_ in range(3):
                        lo = 7 * b_
                        hi = min(7 * (b_ + 1), nslots) - 1
                        if lo >= nslots:
                            continue
                        fin_kk = qbase + hi // 2
                        if kk == fin_kk:
                            nsl = hi - lo + 1
                            def rc(v, b_=b_, nsl=nsl):
                                src = bass.AP(ps_y[b_], 64, [[512, 128], [65, nsl]])
                                return v.reciprocal(rwave[:, 0:nsl], src)
                            vr1 = S.emit("dv", rc,
                                         waits=[("pe", vpe_pv), G["rwave"]],
                                         inc="dv")
                            def tt_(v, b_=b_, nsl=nsl, lo=lo, base=base,
                                    yst_=ystage[par]):
                                o = bass.AP(yst_, base + 64 * lo,
                                            [[n_tt * 128, 128], [64, nsl], [1, 64]])
                                i0 = bass.AP(ps_y[b_], 0,
                                             [[512, 128], [65, nsl], [1, 64]])
                                i1 = bass.AP(rwave, 0, [[8, 128], [1, nsl], [0, 64]])
                                return v.tensor_tensor(o, i0, i1, ALU.mult)
                            vdv = S.emit("dv", tt_,
                                         waits=[("dv", vr1), G["yst%d" % par],
                                                ("q_w2", v_w2done),
                                                ("q_w3", v_w3done)],
                                         inc="dv")
                            G["ps_y%d" % b_] = ("dv", vdv)
                            G["rwave"] = ("dv", vdv)
                    kslot[0] += 1
                    if job_i < len(pend):
                        fn_, tgt_ = pend[job_i]
                        consume(fn_(), tgt_)
                        job_i += 1

                for kk in range(kk_hi):
                    emit_scores(kk)
                    if kk > 0:
                        emit_pv(kk - 1)
                emit_pv(kk_hi - 1)
            while job_i < len(pend):
                fn_, tgt_ = pend[job_i]
                consume(fn_(), tgt_)
                job_i += 1
            G["qk%d" % par] = ("pe", S.val("pe"))
            if s_i == 0:
                emit_wbig_chunk(hp)
            v_evacs = S.val("dv")
            def yst_store(s, s_i=s_i, hp=hp, yst_=ystage[par]):
                src = bass.AP(yst_, 0, [[n_tt * 128, 128], [128, n_tt], [1, 128]])
                dst = ystage_d[s_i, hp].rearrange("(tt p) d -> p tt d", p=128)
                return s.dma_start(out=dst, in_=src)
            if hp < 6:
                vys = S.emit("sy", yst_store, waits=[("dv", v_evacs)],
                             inc="q_ys%d" % par, amt=16)
                G["yst%d" % par] = ("q_ys%d" % par, vys)
                ys_val.append(vys)
            else:
                v_ev67[hp] = v_evacs
            if hp == 6:
                v_xf_free = S.val("pe")
                for hp2 in range(6):
                    S.emit("sy", lambda a, s_i=s_i, hp2=hp2, x_feat=x_feat:
                           a.dma_start(
                               out=x_feat[:, hp2 * T:(hp2 + 1) * T],
                               in_=ystage_d[s_i, hp2][:, :],
                               transpose=True),
                           waits=[("q_ys%d" % (hp2 % 2), ys_val[hp2]),
                                  ("pe", v_xf_free)],
                           inc="q_yf", amt=16)
        nbg = (n_tt + 7) // 8
        y67_banks = {6: [ps_y[0], ps_y[1]], 7: [ps_y[2], ps_g]}
        y67_names = {6: ["ps_y0", "ps_y1"], 7: ["ps_y2", "ps_g"]}
        v_y67 = 0
        for hp2 in (6, 7):
            par2 = hp2 % 2
            vtp = 0
            for gb in range(nbg):
                ntl = min(8, n_tt - gb * 8)
                bank = y67_banks[hp2][gb].bitcast(BF16)
                bname = y67_names[hp2][gb]
                for c in range(ntl):
                    def ytp(t, bank=bank, c=c, gb=gb, ntl=ntl,
                            yst_=ystage[par2]):
                        return t.matmul(
                            bank[:, c * 128:(c + 1) * 128],
                            yst_[:, (gb * 8 + c) * 128:
                                 (gb * 8 + c + 1) * 128],
                            idb_sb[:], is_transpose=True,
                            start=(c == 0), stop=(c == ntl - 1))
                    wl = []
                    if c == 0:
                        wl = [("dv", max(v_ev67[hp2], v_idb)), G[bname]]
                    vtp = S.emit("pe", ytp, waits=wl,
                                 inc="pe" if c == ntl - 1 else None)
                def yev(v, hp2=hp2, gb=gb, bank=bank, ntl=ntl, x_feat=x_feat):
                    return v.tensor_copy(
                        x_feat[:, hp2 * T + gb * 1024:
                               hp2 * T + gb * 1024 + ntl * 128],
                        bank[:, 0:ntl * 128])
                v_y67 = S.emit("dv", yev, waits=[("pe", vtp)], inc="dv")
                G[bname] = ("dv", v_y67)
            G["yst%d" % par2] = ("pe", vtp)
        G["v_all"] = ("pe", S.val("pe"))
        v_attn_pe = S.val("pe")
        st_c.close()
        st_b = ExitStack()
        x_tok = st_b.enter_context(nc.sbuf_tensor(f"x_tok_{s_i}", [128, n_tt * 1024], F32))
        x1stage = st_b.enter_context(nc.sbuf_tensor(f"x1stage_{s_i}", [128, 2 * 1024], BF16))
        yfeat = x_feat
        v_yf = S.val("q_yf")

        xt_q4 = []
        ntq = max(1, n_tt // 4)
        for q4 in range(n_tt // ntq):
            def xt(s, s_i=s_i, q4=q4, ntq=ntq, x_tok=x_tok):
                src = x_in[s_i].rearrange("(tt p) c -> p tt c", p=128)[
                    :, q4 * ntq:(q4 + 1) * ntq]
                dst = x_tok[:].rearrange("p (tt c) -> p tt c", c=1024)[
                    :, q4 * ntq:(q4 + 1) * ntq]
                return s.dma_start(out=dst, in_=src)
            S.emit("po", xt, waits=[G["x_tok"], ("pe", v_attn_pe),
                                    ("q_ys0", S.val("q_ys0")),
                                    ("q_ys1", S.val("q_ys1")),
                                    ("q_w2", v_w2done), ("q_w3", v_w3done),
                                    ("q_xt", S.val("q_xt"))],
                   inc="q_xt", amt=16)
            xt_q4.append(S.val("q_xt"))
        v_xt = S.val("q_xt")

        def linear_ln(win_sb, w_wait, feat_ready_fn, bias_bc, v_bias, x1_mode,
                      v_po_base=0, yfeat=yfeat, x_tok=x_tok):
            stageB = [None] * n_tt

            def stage_a(tt):
                feat_ready = feat_ready_fn(tt)
                reg = ps_s[tt % 2]
                vpe = 0
                for jh in range(2):
                    def mmb(t, tt=tt, jh=jh, reg=reg, bias_row=bias_bc):
                        return t.matmul(
                            reg[:, jh * 512:(jh + 1) * 512],
                            ones_bf[0:1, :],
                            bias_row[0:1, jh * 512:(jh + 1) * 512],
                            start=True, stop=False)
                    w_list = [("q_b", v_brows), ("dv", v_ones)]
                    if jh == 0:
                        w_list.append(G["ps_s%d" % (tt % 2)])
                    S.emit("pe", mmb, waits=w_list)
                for dc in range(CC):
                    for jh in range(2):
                        def mm(t, tt=tt, dc=dc, jh=jh, reg=reg, win_sb=win_sb,
                               yfeat=yfeat):
                            return t.matmul(
                                reg[:, jh * 512:(jh + 1) * 512],
                                yfeat[:, dc * T + tt * 128:dc * T + (tt + 1) * 128],
                                win_sb[:, dc * 1024 + jh * 512:dc * 1024 + (jh + 1) * 512],
                                start=False, stop=(dc == CC - 1))
                        w_list = [feat_ready] + list(w_wait)
                        r = S.emit("pe", mm, waits=w_list,
                                   inc="pe" if (dc == CC - 1 and jh == 1) else None)
                        if r:
                            vpe = r
                par = tt % 2
                sl = tt % NLS
                mv = mv_sb[:, sl * 4:sl * 4 + 4]
                stt = stats_sb[:, sl * 12:sl * 12 + 12]
                def t2(v, tt=tt, par=par, x_tok=x_tok, reg=reg):
                    xt_ = x_tok[:, tt * 1024:(tt + 1) * 1024]
                    return v.tensor_tensor(xt_, xt_, reg, ALU.add)
                v_t2 = S.emit("dv", t2, waits=[("q_xt", xt_q4[tt // ntq]),
                                               ("pe", vpe),
                                               ("po", v_po_base)],
                              inc="dv")
                G["ps_s%d" % par] = ("dv", v_t2)
                def st1(v, tt=tt, x_tok=x_tok, stt=stt):
                    return v.bn_stats(stt[:, 0:6],
                                      x_tok[:, tt * 1024:tt * 1024 + 512])
                S.emit("dv", st1, waits=[("dv", v_t2)], inc="dv")
                def st2(v, tt=tt, x_tok=x_tok, stt=stt):
                    return v.bn_stats(stt[:, 6:12],
                                      x_tok[:, tt * 1024 + 512:(tt + 1) * 1024])
                vs2 = S.emit("dv", st2, inc="dv")
                def ag(v, mv=mv, stt=stt):
                    return v.bn_aggr(mv[:, 0:2], stt[:])
                vag = S.emit("dv", ag,
                             waits=[("dv", vs2), ("ac", ln_guard[sl]["ac"]),
                                    ("dv", ln_guard[sl]["dv"]),
                                    ("po", ln_guard[sl].get("po", 0))], inc="dv")
                def l1(a, mv=mv):
                    return a.activation(mv[:, 2:3], mv[:, 1:2], AF.Ln,
                                        bias=consts[:, 1:2], scale=1.0)
                va1 = S.emit("ac", l1, waits=[("dv", vag)], inc="ac")
                def l2(a, mv=mv):
                    return a.activation(mv[:, 2:3], mv[:, 2:3], AF.Exp,
                                        bias=consts[:, 2:3], scale=-0.5)
                va2 = S.emit("ac", l2, waits=[("ac", va1)], inc="ac")

                def stage_b1(tt=tt, par=par, sl=sl, mv=mv, va2=va2):
                    def nm(v, mv=mv):
                        return v.tensor_scalar(mv[:, 3:4], mv[:, 0:1],
                                               mv[:, 2:3], -1.0, ALU.mult, ALU.mult)
                    vnm = S.emit("dv", nm, waits=[("ac", va2)], inc="dv")
                    if x1_mode:
                        def xb(a, tt=tt, par=par, x_tok=x_tok, mv=mv):
                            return a.activation(
                                x1stage[:, par * 1024:(par + 1) * 1024],
                                x_tok[:, tt * 1024:(tt + 1) * 1024],
                                AF.Identity, bias=mv[:, 3:4], scale=mv[:, 2:3])
                        vxb = S.emit("ac", xb,
                                     waits=[("dv", vnm), G["x1s%d" % par]],
                                     inc="ac")
                        x1xb[tt] = vxb
                        def xf32(g, tt=tt, x_tok=x_tok, mv=mv):
                            return g.tensor_scalar(x_tok[:, tt * 1024:(tt + 1) * 1024],
                                                   x_tok[:, tt * 1024:(tt + 1) * 1024],
                                                   mv[:, 2:3], mv[:, 3:4],
                                                   ALU.mult, ALU.add)
                        vz = S.emit("po", xf32, waits=[("ac", vxb)], inc="po")
                        ln_guard[sl] = {"ac": vxb, "dv": 0, "po": vz}
                    else:
                        def xf32(g, tt=tt, x_tok=x_tok, mv=mv):
                            return g.tensor_scalar(x_tok[:, tt * 1024:(tt + 1) * 1024],
                                                   x_tok[:, tt * 1024:(tt + 1) * 1024],
                                                   mv[:, 2:3], mv[:, 3:4],
                                                   ALU.mult, ALU.add)
                        vz = S.emit("po", xf32, waits=[("dv", vnm)], inc="po")
                        def so(s, s_i=s_i, tt=tt, x_tok=x_tok):
                            return s.dma_start(out=out[s_i, tt * 128:(tt + 1) * 128, :],
                                               in_=x_tok[:, tt * 1024:(tt + 1) * 1024])
                        S.emit("sy", so, waits=[("po", vz)], inc="q_out", amt=16)
                        ln_guard[sl] = {"ac": va2, "dv": 0, "po": vz}
                return stage_b1

            def stage_b2(tt):
                par = tt % 2
                bank = tbank[tt % 4].bitcast(BF16)
                for c in range(CC):
                    def x1tp(t, bank=bank, c=c, par=par):
                        return t.matmul(
                            bank[:, c * 128:(c + 1) * 128],
                            x1stage[:, par * 1024 + c * 128:
                                    par * 1024 + (c + 1) * 128],
                            idb_sb[:], is_transpose=True,
                            start=(c == 0), stop=(c == CC - 1))
                    wl = []
                    if c == 0:
                        wl = [("ac", x1xb[tt]), ("dv", v_idb)]
                        if tt >= 4:
                            wl.append(("ac", x1evac[tt - 4]))
                        else:
                            b = tt % 4
                            wl.append(G["ps_y%d" % b] if b < 3 else G["ps_g"])
                    vtp = S.emit("pe", x1tp, waits=wl,
                                 inc="pe" if c == CC - 1 else None)
                G["x1s%d" % par] = ("pe", vtp)
                def x1ev(a, tt=tt, bank=bank, yfeat=yfeat):
                    o = bass.AP(yfeat, tt * 128,
                                [[CC * T, 128], [T, 8], [1, 128]])
                    return a.copy(o, bank[:, 0:1024])
                x1evac[tt] = S.emit("ac", x1ev, waits=[("pe", vtp)], inc="ac")

            for tt in range(n_tt):
                if x1_mode and tt > 2:
                    stage_b2(tt - 3)
                if tt > 0:
                    stageB[tt - 1]()
                stageB[tt] = stage_a(tt)
            if x1_mode and n_tt >= 3:
                stage_b2(n_tt - 3)
            stageB[n_tt - 1]()
            if x1_mode:
                for tt in range(max(0, n_tt - 2), n_tt):
                    stage_b2(tt)

        x1evac = {}
        x1xb = {}
        linear_ln(Wp_sb, [("q_w", v_w), ("dv", v_y67)],
                  lambda tt: ("q_yf", v_yf),
                  bp_row, 0, True)
        for b in range(4):
            last = max(t for t in x1evac if t % 4 == b)
            gk = "ps_y%d" % b if b < 3 else "ps_g"
            G[gk] = ("ac", x1evac[last])
        linear_ln(Wbig_sb, v_wbig_holder["v"],
                  lambda tt: ("ac", x1evac[tt]),
                  bf_row, 0, False, v_po_base=S.val("po"))
        G["yfeat"] = ("pe", S.val("pe"))
        G["x_feat"] = ("pe", S.val("pe"))
        G["x_tok"] = ("q_out", S.val("q_out"))
        st_b.close()
        st_seq.close()

    v_out_final = S.val("q_out")

    with st:
        with nc.Block() as block:
            @block.tensor
            def _(t):
                S.run("pe", t, sems)

            @block.scalar
            def _(a):
                S.run("ac", a, sems)

            @block.vector
            def _(v):
                S.run("dv", v, sems)

            @block.gpsimd
            def _(g):
                S.run("po", g, sems)

            @block.sync
            def _(s):
                S.run("sy", s, sems)
                s.wait_ge(sems["q_out"], v_out_final)

    return nc


_NC_CACHE = {}


def kernel(**inputs):
    x = np.asarray(inputs["L0_attn_x"], dtype=np.float32)
    w_attn = np.asarray(inputs["L0_attn_W_attn"], dtype=np.float32)
    b_attn = np.asarray(inputs["L0_attn_b_attn"], dtype=np.float32)
    w_proj = np.asarray(inputs["L0_attn_W_proj"], dtype=np.float32)
    b_proj = np.asarray(inputs["L0_attn_b_proj"], dtype=np.float32)
    g1r = np.asarray(inputs["L0_ffn_G1_R"], dtype=np.float32)
    g2r = np.asarray(inputs["L0_ffn_G2_R"], dtype=np.float32)
    g1i = np.asarray(inputs["L0_ffn_G1_I"], dtype=np.float32)
    g2i = np.asarray(inputs["L0_ffn_G2_I"], dtype=np.float32)
    b_real = np.asarray(inputs["L0_ffn_bias_real"], dtype=np.float32)
    b_imag = np.asarray(inputs["L0_ffn_bias_imag"], dtype=np.float32)

    mask = np.triu(np.ones((128, 128), dtype=np.float32)).astype(bfdt)
    eye = np.eye(128, dtype=np.float32)

    if "nc" not in _NC_CACHE:
        _NC_CACHE["nc"] = build(2, 16)
    nc = _NC_CACHE["nc"]

    in_maps = []
    for core in range(8):
        in_maps.append({
            "x_in": np.ascontiguousarray(x[core * 2:(core + 1) * 2]),
            "w_attn": w_attn, "b_attn": b_attn,
            "w_proj": w_proj, "b_proj": b_proj,
            "g1r": g1r, "g2r": g2r, "g1i": g1i, "g2i": g2i,
            "b_real": b_real, "b_imag": b_imag,
            "mask01": mask, "ident": eye,
        })
    res = run_bass_kernel_spmd(nc, in_maps, core_ids=list(range(8)))
    return np.concatenate([r["out"] for r in res.results], axis=0)


# version 49 (final)
# speedup vs baseline: 1.0743x; 1.0021x over previous
import numpy as np
import ml_dtypes
from contextlib import ExitStack

import concourse.bass as bass
import concourse.mybir as mybir
from concourse.bass_utils import run_bass_kernel_spmd

F32 = mybir.dt.float32
BF16 = mybir.dt.bfloat16
AF = mybir.ActivationFunctionType
ALU = mybir.AluOpType
bfdt = ml_dtypes.bfloat16

C = 1024
NPAIR = 8
CC = 8
EXP_OFF = -12.0
LN_EPS = 1e-5

ENGINES = ("pe", "ac", "dv", "po", "sy")


class Sched:

    def __init__(self):
        self.ops = {e: [] for e in ENGINES}
        self.cnt = {}

    def emit(self, eng, fn, waits=(), inc=None, amt=1):
        self.ops[eng].append((tuple(waits), fn, inc, amt))
        if inc is not None:
            self.cnt[inc] = self.cnt.get(inc, 0) + amt
            return self.cnt[inc]
        return None

    def val(self, sem):
        return self.cnt.get(sem, 0)

    def run(self, eng_name, engine, sems):
        floors = {}
        for waits, fn, inc, amt in self.ops[eng_name]:
            for s, v in waits:
                if v and v > floors.get(s, 0):
                    engine.wait_ge(sems[s], v)
                    floors[s] = v
            instr = fn(engine)
            if inc is not None:
                instr.then_inc(sems[inc], amt)


def build(n_seq=2, n_tt=16):
    T = n_tt * 128
    GS = min(8, n_tt)
    NG = n_tt // GS
    NTG = T // 512
    assert NTG >= 1
    nc = bass.Bass("TRN2", target_bir_lowering=False)

    x_in = nc.dram_tensor("x_in", [n_seq, T, C], F32, kind="ExternalInput")
    w_attn = nc.dram_tensor("w_attn", [C, 3 * C], F32, kind="ExternalInput")
    b_attn = nc.dram_tensor("b_attn", [3 * C], F32, kind="ExternalInput")
    w_proj = nc.dram_tensor("w_proj", [C, C], F32, kind="ExternalInput")
    b_proj = nc.dram_tensor("b_proj", [C], F32, kind="ExternalInput")
    g1r = nc.dram_tensor("g1r", [1, 32, 4, 32], F32, kind="ExternalInput")
    g2r = nc.dram_tensor("g2r", [4, 16, 1, 16], F32, kind="ExternalInput")
    g1i = nc.dram_tensor("g1i", [1, 32, 4, 32], F32, kind="ExternalInput")
    g2i = nc.dram_tensor("g2i", [4, 16, 1, 16], F32, kind="ExternalInput")
    b_real = nc.dram_tensor("b_real", [512], F32, kind="ExternalInput")
    b_imag = nc.dram_tensor("b_imag", [512], F32, kind="ExternalInput")
    mask01 = nc.dram_tensor("mask01", [128, 128], BF16, kind="ExternalInput")
    ident = nc.dram_tensor("ident", [128, 128], F32, kind="ExternalInput")
    out = nc.dram_tensor("out", [n_seq, T, C], F32, kind="ExternalOutput")

    ystage_d = nc.dram_tensor("ystage_d", [n_seq, NPAIR, T, 128], BF16)
    battn_d = nc.dram_tensor("battn_d", [16, 128], BF16)
    w2r_d = nc.dram_tensor("w2r_d", [512, 512], BF16)
    w2i_d = nc.dram_tensor("w2i_d", [512, 512], BF16)

    st = ExitStack()

    def sb(name, shape, dtype):
        return st.enter_context(nc.sbuf_tensor(name, shape, dtype))

    def psum(name, shape):
        return st.enter_context(nc.psum_tensor(name, shape, F32))

    W_sb = sb("W_sb", [128, CC * 3072], BF16)
    Wp_sb = sb("Wp_sb", [128, CC * 1024], BF16)
    Wbig_sb = sb("Wbig_sb", [128, CC * 1024], BF16)
    mask_sb = sb("mask_sb", [128, 128], BF16)
    bqk = sb("bqk", [128, 16], F32)
    bv_bc = sb("bv_bc", [128, 1024], BF16)
    bp_bc = sb("bp_bc", [128, 1024], BF16)
    bf_bc = sb("bf_bc", [128, 1024], BF16)
    ones128 = sb("ones128", [128, 128], BF16)
    ones_bf = sb("ones_bf", [1, 128], BF16)
    consts = sb("consts", [128, 4], F32)
    rwave = sb("rwave", [128, 8], F32)
    NLS = 4
    mv_sb = sb("mv_sb", [128, 4 * NLS], F32)
    stats_sb = sb("stats_sb", [128, 12 * NLS], F32)

    bqk_bf = sb("bqk_bf", [128, 16], BF16)
    idf_sb = sb("idf_sb", [128, 128], F32)
    idb_sb = sb("idb_sb", [128, 128], BF16)
    xstage = sb("xstage", [128, 3 * 1024], F32)
    row_bf = sb("row_bf", [1, 1024], BF16)
    wts_bf = sb("wts_bf", [128, 512], BF16)
    p_diag = [sb(f"pd{i}", [128, 256], BF16) for i in range(2)]
    st_w = ExitStack()
    wstage = st_w.enter_context(nc.sbuf_tensor("wstage", [128, 4 * 3072], F32))
    g1_sb = st_w.enter_context(nc.sbuf_tensor("g1_sb", [4, 2048], F32))
    g2_sb = st_w.enter_context(nc.sbuf_tensor("g2_sb", [4, 512], F32))
    st_w.enter_context(nc.sbuf_tensor("w2pad", [128, 8832], F32))
    w2big = st_w.enter_context(nc.sbuf_tensor("w2big", [128, 2 * 2048], BF16))

    ps_big = psum("ps_big", [128, 2048])
    ps_s = [ps_big[:, 0:1024], ps_big[:, 1024:2048]]
    ps_y = [psum("ps_y0", [128, 512]), psum("ps_y1", [128, 512]),
            psum("ps_y2", [128, 512])]
    ps_g = psum("ps_g", [128, 512])

    sem_names = ["pe", "ac", "dv", "po",
                 "q_w", "q_wa", "q_b", "q_wt", "q_g", "q_w2", "q_w3", "q_xc",
                 "q_xs0", "q_xs1", "q_xb0", "q_xb1", "q_xf",
                 "q_ys0", "q_ys1", "q_yf",
                 "q_x1s0", "q_x1s1", "q_x1f", "q_xt", "q_out", "q_m", "q_bt",
                 "q_wn"]
    sems = {n: st.enter_context(nc.semaphore(n)) for n in sem_names}

    S = Sched()

    def emit_qm(fn, waits=()):
        w = [("q_m", S.val("q_m"))] + list(waits)
        return S.emit("ac", fn, waits=w, inc="q_m", amt=16)
    G = {"ps_s0": ("dv", 0), "ps_s1": ("dv", 0),
         "ps_y0": ("dv", 0), "ps_y1": ("dv", 0), "ps_y2": ("dv", 0),
         "ps_g": ("dv", 0),
         "p0": ("pe", 0), "p1": ("pe", 0), "p2": ("pe", 0),
         "qk0": ("pe", 0), "qk1": ("pe", 0),
         "yst0": ("q_ys0", 0), "yst1": ("q_ys1", 0),
         "v_all": ("pe", 0), "x_feat": ("pe", 0), "yfeat": ("pe", 0),
         "x_tok": ("q_out", 0),
         "tmp0": ("dv", 0), "tmp1": ("dv", 0),
         "x1s0": ("q_x1s0", 0), "x1s1": ("q_x1s0", 0),
         "rwave": ("dv", 0), "w2_sb": ("q_m", 0), "g12": ("pe", 0),
         "pd0": ("pe", 0), "pd1": ("pe", 0)}
    ln_guard = [{"ac": 0, "dv": 0, "po": 0} for _ in range(4)]

    S.emit("dv", lambda v: v.memset(consts[:, 0:1], EXP_OFF))
    S.emit("dv", lambda v: v.memset(consts[:, 1:2], LN_EPS))
    S.emit("dv", lambda v: v.memset(consts[:, 2:3], 0.0))
    S.emit("dv", lambda v: v.memset(consts[:, 3:4], 1.0 / 128.0), inc="dv")
    v_consts = S.val("dv")
    S.emit("dv", lambda v: v.memset(ones_bf[:], 1.0), inc="dv")
    v_ones = S.val("dv")

    def bias_bcast(dst, wait_qb, scale128=False):
        vdv_last = 0
        vpe_last = 0
        for jh in range(2):
            def mm(t, jh=jh, wait_qb=wait_qb):
                return t.matmul(ps_g[:], ones_bf[0:1, :],
                                row_bf[0:1, jh * 512:(jh + 1) * 512],
                                start=True, stop=True)
            vpe_last = S.emit("pe", mm,
                              waits=[("q_b", wait_qb), ("dv", v_ones), G["ps_g"]],
                              inc="pe")
            if scale128:
                def cp(v, jh=jh, dst=dst):
                    return v.tensor_scalar(dst[:, jh * 512:(jh + 1) * 512],
                                           ps_g[:], consts[:, 3:4], None,
                                           ALU.mult)
            else:
                def cp(v, jh=jh, dst=dst):
                    return v.tensor_copy(dst[:, jh * 512:(jh + 1) * 512],
                                         ps_g[:])
            vdv_last = S.emit("dv", cp,
                              waits=[("pe", vpe_last), ("dv", v_consts)],
                              inc="dv")
            G["ps_g"] = ("dv", vdv_last)
        return vdv_last, vpe_last

    S.emit("dv", lambda v: v.memset(ones128[:], 1.0), inc="dv")
    v_ones128 = S.val("dv")
    v_qb = S.emit("po", lambda g: g.dma_start(out=row_bf[0:1, :],
                                              in_=b_attn[2048:3072]),
                  inc="q_b", amt=16)
    v_bvbc, v_pe_bv = bias_bcast(bv_bc, v_qb)
    v_qb = S.emit("po", lambda g: g.dma_start(out=row_bf[0:1, :],
                                              in_=b_proj[:]),
                  waits=[("q_b", S.val("q_b")), ("pe", v_pe_bv)],
                  inc="q_b", amt=16)
    v_bpbc, v_pe_bp = bias_bcast(bp_bc, v_qb, scale128=True)
    S.emit("po", lambda g: g.dma_start(out=row_bf[0:1, 0:512], in_=b_real[:]),
           waits=[("q_b", S.val("q_b")), ("pe", v_pe_bp)], inc="q_b", amt=16)
    v_qb = S.emit("po", lambda g: g.dma_start(out=row_bf[0:1, 512:1024],
                                              in_=b_imag[:]),
                  waits=[("q_b", S.val("q_b"))], inc="q_b", amt=16)
    v_bfbc, _ = bias_bcast(bf_bc, v_qb, scale128=True)
    v_bbc = max(v_bpbc, v_bfbc)
    v_brows = v_qb

    for fi, (gd1, gd2) in enumerate([(g1r, g2r), (g1i, g2i)]):
        S.emit("sy", lambda s, gd1=gd1, fi=fi: s.dma_start(
            out=g1_sb[:, fi * 1024:(fi + 1) * 1024],
            in_=gd1[0].rearrange("b c d -> c b d")), inc="q_g", amt=16)
        S.emit("sy", lambda s, gd2=gd2, fi=fi: s.dma_start(
            out=g2_sb[:, fi * 256:(fi + 1) * 256],
            in_=gd2.rearrange("c e one g -> c (e one g)")), inc="q_g", amt=16)
    v_g = S.val("q_g")


    for fi, wdst in enumerate([w2r_d, w2i_d]):
        vpe = 0
        for m in range(8):
            def mm(t, m=m, fi=fi):
                return t.matmul(ps_big[:, m * 256:(m + 1) * 256],
                                g1_sb[:, fi * 1024 + m * 128:fi * 1024 + (m + 1) * 128],
                                g2_sb[:, fi * 256:(fi + 1) * 256],
                                start=True, stop=True)
            w_list = [("q_g", v_g)]
            if m == 0:
                w_list += [G["ps_s0"], G["ps_s1"]]
                if fi == 1:
                    w_list.append(("dv", v_w2cp))
            vpe = S.emit("pe", mm, waits=w_list, inc="pe" if m == 7 else None)
        def cp(v, fi=fi):
            src_ = bass.AP(ps_big, 0, [[2048, 128], [256, 8], [1, 16], [16, 16]])
            return v.tensor_copy(w2big[:, fi * 2048:(fi + 1) * 2048], src_)
        v_w2cp = S.emit("dv", cp, waits=[("pe", vpe)], inc="dv")
        G["ps_s0"] = ("dv", v_w2cp)
        G["ps_s1"] = ("dv", v_w2cp)
        for m in range(8):
            sem = "q_w2" if m % 2 == 0 else "q_w3"
            ring = "sy"
            def stf(g, m=m, fi=fi, wdst=wdst):
                o = wdst.rearrange("(d g) (b e) -> b d g e", g=16, e=16)[4 * m:4 * (m + 1)]
                return g.dma_start(out=o, in_=w2big[:, fi * 2048 + m * 256:
                                                    fi * 2048 + (m + 1) * 256])
            S.emit(ring, stf, waits=[("dv", v_w2cp), (sem, S.val(sem))],
                   inc=sem, amt=16)
    v_w2done = S.val("q_w2")
    v_w3done = S.val("q_w3")

    prev = ([], 0, 0)
    for wave in range(2):
        w_extra = []
        if wave == 1:
            w_extra = [("ac", S.val("ac")), ("dv", S.val("dv"))]
        for i in range(4):
            cc = wave * 4 + i
            S.emit("ac", lambda a, cc=cc, i=i: a.dma_start(
                out=wstage[:, i * 3072:(i + 1) * 3072],
                in_=w_attn[cc * 128:(cc + 1) * 128, :]),
                waits=w_extra, inc="q_wa", amt=16)
        v_wa_ld = S.val("q_wa")
        for i in range(4):
            cc = wave * 4 + i
            if i % 2 == 0:
                S.emit("ac", lambda a, cc=cc, i=i: a.copy(
                    W_sb[:, cc * 3072:(cc + 1) * 3072],
                    wstage[:, i * 3072:(i + 1) * 3072]),
                    waits=[("q_wa", v_wa_ld)], inc="ac")
            else:
                S.emit("dv", lambda v, cc=cc, i=i: v.tensor_copy(
                    W_sb[:, cc * 3072:(cc + 1) * 3072],
                    wstage[:, i * 3072:(i + 1) * 3072]),
                    waits=[("q_wa", v_wa_ld)], inc="dv")
    v_wa_ac = S.val("ac")
    v_wa_dv = S.val("dv")



    S.emit("po", lambda g: g.dma_start(
        out=battn_d[:], in_=b_attn[0:2048].rearrange("(a b) -> a b", b=128)),
        inc="q_bt", amt=16)
    v_battn = S.val("q_bt")



    v_mask_ld = emit_qm(lambda s: s.dma_start(out=mask_sb[:], in_=mask01[:]))
    v_id = emit_qm(lambda s: s.dma_start(out=idf_sb[:], in_=ident[:]))
    v_idb = S.emit("dv", lambda v: v.tensor_copy(idb_sb[:], idf_sb[:]),
                   waits=[("q_m", v_id)], inc="dv")
    v_bqkbf = emit_qm(lambda s: s.dma_start(out=bqk_bf[:], in_=battn_d[:],
                                            transpose=True),
                      waits=[("q_bt", v_battn)])


    v_bqk = S.emit("ac", lambda a: a.copy(bqk[:], bqk_bf[:]),
                   waits=[("q_m", v_bqkbf)], inc="ac")


    wbig_state = {"guard": ("ac", 0)}

    def emit_wbig_chunk(cc2):
        r0 = (cc2 % 4) * 128
        wq = [("q_w2", v_w2done), ("q_w3", v_w3done)]
        if cc2 < 4:
            S.emit("po", lambda g, r0=r0, cc2=cc2: g.dma_start(
                out=Wbig_sb[:, cc2 * 1024:cc2 * 1024 + 512],
                in_=w2r_d[r0:r0 + 128, :]), waits=wq, inc="q_wt", amt=16)
            S.emit("po", lambda g, r0=r0, cc2=cc2: g.dma_start(
                out=Wbig_sb[:, cc2 * 1024 + 512:(cc2 + 1) * 1024],
                in_=w2i_d[r0:r0 + 128, :]), inc="q_wt", amt=16)
        else:
            v_wt = S.emit("po", lambda g, r0=r0: g.dma_start(
                out=wts_bf[:, 0:512], in_=w2i_d[r0:r0 + 128, :]),
                waits=wq + [wbig_state["guard"]], inc="q_wn", amt=16)
            S.emit("po", lambda g, r0=r0, cc2=cc2: g.dma_start(
                out=Wbig_sb[:, cc2 * 1024 + 512:(cc2 + 1) * 1024],
                in_=w2r_d[r0:r0 + 128, :]), inc="q_wt", amt=16)
            vneg = S.emit("ac", lambda a, cc2=cc2: a.activation(
                Wbig_sb[:, cc2 * 1024:cc2 * 1024 + 512],
                wts_bf[:, 0:512], AF.Identity,
                bias=consts[:, 2:3], scale=-1.0),
                waits=[("q_wn", v_wt), ("dv", v_consts)], inc="ac")
            wbig_state["guard"] = ("ac", vneg)
        if cc2 == CC - 1:
            v_wbig_holder["v"] = [("ac", S.val("ac")),
                                  ("q_wt", S.val("q_wt"))]

    v_wbig_holder = {}
    for dc in range(CC):
        S.emit("po", lambda g, dc=dc: g.dma_start(
            out=Wp_sb[:, dc * 1024:(dc + 1) * 1024],
            in_=w_proj[dc * 128:(dc + 1) * 128, :], max_dma_last_dim=8192),
            inc="q_w", amt=16)
    v_w = S.val("q_w")
    st_w.close()
    v_wa_dv = max(v_wa_dv, S.val("dv"))
    v_setup_pe = S.val("pe")

    for s_i in range(n_seq):
        st_seq = ExitStack()
        x_feat = st_seq.enter_context(
            nc.sbuf_tensor(f"x_feat_{s_i}", [128, CC * T], BF16))
        st_c = ExitStack()

        def sbc(name, shape, dtype):
            return st_c.enter_context(nc.sbuf_tensor(name, shape, dtype))

        v_all = sbc(f"v_all_{s_i}", [128, n_tt * 1040], BF16)
        qp = [sbc(f"qp{i}_{s_i}", [128, T], BF16) for i in range(2)]
        kp = [sbc(f"kp{i}_{s_i}", [128, T], BF16) for i in range(2)]
        p_sb = [sbc(f"p{i}_{s_i}", [128, 2048], BF16) for i in range(3)]
        ystage = [sbc(f"yst{i}_{s_i}", [128, n_tt * 128], BF16) for i in range(2)]
        S.emit("dv", lambda v, v_all=v_all: v.memset(
            bass.AP(v_all, 64, [[n_tt * 1040, 128], [65, n_tt * 16]]), 1.0),
            waits=[G["x_tok"], G["v_all"], ("ac", v_wa_ac), ("dv", v_wa_dv),
                   ("pe", v_setup_pe)],
            inc="dv")
        v_vones = S.val("dv")
        xslot_sem = ["q_xc", "q_xs0", "q_xs1"]
        tbank = [ps_y[0], ps_y[1], ps_y[2], ps_g]
        xf_tile = []
        tp_vals = []
        v_ld = []

        def emit_xload(tt):
            sl = tt % 3
            wl = []
            if tt >= 3:
                wl.append(("pe", tp_vals[tt - 3]))
            elif s_i > 0:
                wl.append(("pe", prev_tp[tt - 3]))
            S.emit("po", lambda g, s_i=s_i, tt=tt, sl=sl: g.dma_start(
                out=xstage[:, sl * 1024:(sl + 1) * 1024],
                in_=x_in[s_i][tt * 128:(tt + 1) * 128, :]),
                waits=wl, inc=xslot_sem[sl], amt=16)
            v_ld.append(S.val(xslot_sem[sl]))

        def emit_xtp(tt):
            sl = tt % 3
            for c in range(CC):
                bank = tbank[(2 * tt + c // 4) % 4]
                q = c % 4
                def tp(t, bank=bank, q=q, sl=sl, c=c):
                    return t.matmul(bank[:, q * 128:(q + 1) * 128],
                                    xstage[:, sl * 1024 + c * 128:
                                           sl * 1024 + (c + 1) * 128],
                                    idf_sb[:], is_transpose=True,
                                    start=(q == 0), stop=(q == 3))
                wl = []
                if c in (0, 4):
                    wl = [(xslot_sem[sl], v_ld[tt]), ("q_m", v_id)]
                    if tt >= 2:
                        wl.append(("dv", xf_tile[tt - 2]))
                    else:
                        b = (2 * tt + c // 4) % 4
                        wl.append(G["ps_y%d" % b] if b < 3 else G["ps_g"])
                r = S.emit("pe", tp, waits=wl,
                           inc="pe" if c == CC - 1 else None)
            tp_vals.append(r)

        def emit_xevac(tt):
            for half in range(2):
                bank = tbank[(2 * tt + half) % 4]
                def ev(v, tt=tt, half=half, bank=bank, x_feat=x_feat):
                    o = bass.AP(x_feat, tt * 128 + half * 4 * T,
                                [[CC * T, 128], [T, 4], [1, 128]])
                    return v.tensor_copy(o, bank[:])
                r = S.emit("dv", ev,
                           waits=[("pe", tp_vals[tt]), G["x_feat"],
                                  ("ac", v_wa_ac), ("dv", v_wa_dv),
                                  ("pe", v_setup_pe)],
                           inc="dv" if half == 1 else None)
            xf_tile.append(r)

        def emit_vmm(tt):
            reg = ps_s[tt % 2]
            vpe = 0
            for cc in range(CC):
                for jh in range(2):
                    def mm(t, tt=tt, cc=cc, jh=jh, reg=reg, x_feat=x_feat):
                        return t.matmul(
                            reg[:, jh * 512:(jh + 1) * 512],
                            x_feat[:, cc * T + tt * 128:cc * T + (tt + 1) * 128],
                            W_sb[:, cc * 3072 + 2048 + jh * 512:
                                 cc * 3072 + 2048 + (jh + 1) * 512],
                            start=(cc == 0), stop=(cc == CC - 1))
                    w_list = [("dv", xf_tile[tt]),
                              ("ac", v_wa_ac), ("dv", v_wa_dv)]
                    if cc == 0 and jh == 0:
                        w_list.append(G["ps_s%d" % (tt % 2)])
                    r = S.emit("pe", mm, waits=w_list,
                               inc="pe" if (cc == CC - 1 and jh == 1) else None)
                    if r:
                        vpe = r
            def ev(v, tt=tt, half=tt % 2, v_all=v_all):
                o = bass.AP(v_all, tt * 1040, [[n_tt * 1040, 128], [65, 16], [1, 64]])
                i0 = bass.AP(ps_big, half * 1024, [[2048, 128], [64, 16], [1, 64]])
                i1 = bass.AP(bv_bc, 0, [[1024, 128], [64, 16], [1, 64]])
                return v.tensor_tensor(o, i0, i1, ALU.add)
            vdv = S.emit("dv", ev,
                         waits=[("pe", vpe),
                                ("dv", max(v_bvbc, v_vones)),
                                G["v_all"]],
                         inc="dv")
            G["ps_s%d" % (tt % 2)] = ("dv", vdv)
            return vdv

        for tt in range(min(3, n_tt)):
            emit_xload(tt)
        for tt in range(min(4, n_tt)):
            emit_xtp(tt)
            emit_xevac(tt)
            if tt + 3 < n_tt:
                emit_xload(tt + 3)
        v_vall_ready = 0
        for tt in range(n_tt):
            v_vall_ready = emit_vmm(tt)
            if tt + 4 < n_tt:
                emit_xtp(tt + 4)
                emit_xevac(tt + 4)
            if tt + 7 < n_tt:
                emit_xload(tt + 7)
        prev_tp = tp_vals
        G["ps_y0"] = ("dv", xf_tile[n_tt - 2])
        G["ps_y1"] = ("dv", xf_tile[n_tt - 2])
        G["ps_y2"] = ("dv", xf_tile[n_tt - 1])
        G["ps_g"] = ("dv", xf_tile[n_tt - 1])

        def make_qk_jobs(hp, alt_bank=False):
            jobs = []
            par = hp % 2
            ji = [0]
            for tg in range(NTG):
                for which in range(2):
                    j0 = which * 1024 + 128 * hp
                    dstbuf = qp[par] if which == 0 else kp[par]
                    bcol = which * 8 + hp
                    bname = ("ps_y0" if (alt_bank and (2 * tg + which) % 2)
                             else "ps_g")
                    bank = ps_y[0] if bname == "ps_y0" else ps_g
                    def half(lo, which=which, j0=j0, dstbuf=dstbuf, bcol=bcol,
                             tg=tg, par=par, bname=bname, bank=bank):
                        vpe = 0
                        for cc in range(lo, lo + CC // 2):
                            def mm(t, cc=cc, j0=j0, tg=tg, x_feat=x_feat,
                                   bank=bank):
                                return t.matmul(
                                    bank[:],
                                    W_sb[:, cc * 3072 + j0:cc * 3072 + j0 + 128],
                                    x_feat[:, cc * T + tg * 512:cc * T + (tg + 1) * 512],
                                    start=(cc == 0), stop=(cc == CC - 1))
                            w_list = [("dv", xf_tile[4 * tg + 3]),
                                      ("ac", v_wa_ac), ("dv", v_wa_dv)]
                            if cc == 0:
                                w_list.append(G[bname])
                            r = S.emit("pe", mm, waits=w_list,
                                       inc="pe" if cc == CC - 1 else None)
                            if r:
                                vpe = r
                        if lo > 0:
                            def evq(v, dstbuf=dstbuf, tg=tg, bcol=bcol,
                                    bank=bank):
                                return v.tensor_scalar(
                                    dstbuf[:, tg * 512:(tg + 1) * 512], bank[:],
                                    bqk[:, bcol:bcol + 1], None, ALU.add)
                            vdv = S.emit("dv", evq,
                                         waits=[("pe", vpe), ("ac", v_bqk),
                                                G["qk%d" % par]], inc="dv")
                            G[bname] = ("dv", vdv)
                            return (tg if which == 1 else None, vdv)
                        return None
                    jobs.append(lambda half=half: half(0))
                    jobs.append(lambda half=half: half(CC // 2))
            return jobs

        tg_need = [((g + 1) * GS * 128) // 512 - 1 for g in range(NG)]

        def consume(job_ret, vq):
            if job_ret is None:
                return
            tg_fin, vdv = job_ret
            if tg_fin is None:
                return
            for g in range(NG):
                if tg_need[g] == tg_fin:
                    vq[g] = vdv

        qk_done = {}
        ys_val = []
        v_ev67 = {}
        vq0 = [0] * NG
        for j in make_qk_jobs(0, alt_bank=True):
            consume(j(), vq0)
        qk_done[0] = vq0

        for hp in range(NPAIR):
            par = hp % 2
            vq_ready = qk_done[hp]
            pend = make_qk_jobs(hp + 1) if hp + 1 < NPAIR else []
            vq_next = [0] * NG
            qk_done[hp + 1] = vq_next
            job_i = 0
            kslot = [0]
            spar = 0
            for g_ in range(NG):
                base = 128 * GS * g_
                qbase = GS * g_
                kk_hi = qbase + GS
                nslots = 2 * GS
                first_of_bank = {}
                exp_of_kk = {}
                exp_chunks = {}
                mask_of_kk = {}

                def emit_scores(kk):
                    nonlocal spar
                    tail_start = max(base, 128 * kk)
                    tail_end = base + 128 * GS
                    pos = tail_start
                    while pos < tail_end:
                        ln = min(512, tail_end - pos)
                        half = spar
                        for e in range(2):
                            def mm(t, e=e, kk=kk, pos=pos, ln=ln, half=half,
                                   kpb=kp[par], qpb=qp[par]):
                                off = half * 1024 + e * 512
                                return t.matmul(
                                    ps_big[:, off:off + ln],
                                    kpb[e * 64:(e + 1) * 64,
                                        kk * 128:(kk + 1) * 128],
                                    qpb[e * 64:(e + 1) * 64, pos:pos + ln],
                                    start=True, stop=True)
                            w_list = []
                            if e == 0:
                                w_list = [("dv", vq_ready[g_]),
                                          G["ps_s%d" % half]]
                            r = S.emit("pe", mm, waits=w_list,
                                       inc="pe" if e == 1 else None)
                            if r:
                                vpe = r
                        def ex(a, kk=kk, pos=pos, ln=ln, half=half, base=base,
                               psel=p_sb[kk % 3]):
                            i = bass.AP(ps_big, half * 1024,
                                        [[2048, 128], [512, 2], [1, ln]])
                            o = bass.AP(psel, pos - base,
                                        [[2048, 128], [1024, 2], [1, ln]])
                            return a.activation(o, i, AF.Exp, bias=consts[:, 0:1],
                                                scale=0.125)
                        vac = S.emit("ac", ex,
                                     waits=[("pe", vpe), ("dv", v_consts),
                                            G["p%d" % (kk % 3)]], inc="ac")
                        G["ps_s%d" % half] = ("ac", vac)
                        exp_of_kk[kk] = vac
                        exp_chunks.setdefault(kk, []).append(vac)
                        spar ^= 1
                        pos += ln
                    if kk >= qbase:
                        def mk(v, kk=kk, base=base, psel=p_sb[kk % 3],
                               pd=p_diag[kk % 2]):
                            pp = bass.AP(psel, 128 * kk - base,
                                         [[2048, 128], [1024, 2], [1, 128]])
                            po_ = bass.AP(pd, 0, [[256, 128], [128, 2], [1, 128]])
                            mm_ = bass.AP(mask_sb, 0, [[128, 128], [0, 2], [1, 128]])
                            return v.tensor_tensor(po_, pp, mm_, ALU.mult)
                        mask_of_kk[kk] = S.emit(
                            "dv", mk, waits=[("ac", exp_of_kk[kk]),
                                             ("q_m", v_mask_ld),
                                             G["pd%d" % (kk % 2)]], inc="dv")

                def emit_pv(kk):
                    nonlocal job_i
                    qql_lo = max(0, kk - qbase)
                    pv_list = [(e, qql) for e in range(2)
                               for qql in range(qql_lo, GS)
                               if not (kk >= qbase and qql == qql_lo)]
                    nondiag_last = pv_list[-1] if pv_list else None
                    if kk >= qbase:
                        pv_list += [(0, qql_lo), (1, qql_lo)]
                    vpe_nd = 0
                    vpe_pv = 0
                    for e, qql in pv_list:
                        s3 = 2 * qql + e
                        b_ = min(s3 // 7, 2)
                        off = (s3 - 7 * b_) * 65
                        first = not first_of_bank.get(b_, False)
                        first_of_bank[b_] = True
                        hi_abs = nslots - 1 if b_ == 2 else                             min(7 * (b_ + 1), nslots) - 1
                        is_bank_last = (s3 == hi_abs) and (kk == qbase + qql)
                        is_diag = (kk >= qbase and qql == qql_lo)
                        if is_diag:
                            lhs_ap = bass.AP(p_diag[kk % 2], e * 128,
                                             [[256, 128], [1, 128]])
                        else:
                            lhs_ap = p_sb[kk % 3][:, e * 1024 + qql * 128:
                                                  e * 1024 + (qql + 1) * 128]
                        def mm(t, e=e, kk=kk, b_=b_, off=off, hp=hp,
                               first=first, is_bank_last=is_bank_last,
                               lhs_ap=lhs_ap, v_all=v_all):
                            vcol = kk * 1040 + (2 * hp + e) * 65
                            return t.matmul(
                                ps_y[b_][:, off:off + 65],
                                lhs_ap,
                                v_all[:, vcol:vcol + 65],
                                start=first, stop=is_bank_last)
                        ts_ = max(base, 128 * kk)
                        ch = min((base + qql * 128 - ts_) // 512,
                                 len(exp_chunks[kk]) - 1)
                        w_list = [("ac", exp_chunks[kk][ch])]
                        if is_diag:
                            w_list.append(("dv", mask_of_kk[kk]))
                        if first:
                            w_list.append(G["ps_y%d" % b_])
                            w_list.append(("dv", v_vall_ready))
                        is_last_emitted = (e, qql) == pv_list[-1]
                        is_nd_last = (e, qql) == nondiag_last
                        r = S.emit("pe", mm, waits=w_list,
                                   inc="pe" if (is_last_emitted or is_nd_last)
                                   else None)
                        if r and is_nd_last:
                            vpe_nd = r
                        if r:
                            vpe_pv = r
                    G["p%d" % (kk % 3)] = ("pe",
                                           vpe_nd if nondiag_last else vpe_pv)
                    if kk >= qbase:
                        G["pd%d" % (kk % 2)] = ("pe", vpe_pv)
                    for b_ in range(3):
                        lo = 7 * b_
                        hi = min(7 * (b_ + 1), nslots) - 1
                        if lo >= nslots:
                            continue
                        fin_kk = qbase + hi // 2
                        if kk == fin_kk:
                            nsl = hi - lo + 1
                            def rc(v, b_=b_, nsl=nsl):
                                src = bass.AP(ps_y[b_], 64, [[512, 128], [65, nsl]])
                                return v.reciprocal(rwave[:, 0:nsl], src)
                            vr1 = S.emit("dv", rc,
                                         waits=[("pe", vpe_pv), G["rwave"]],
                                         inc="dv")
                            def tt_(v, b_=b_, nsl=nsl, lo=lo, base=base,
                                    yst_=ystage[par]):
                                o = bass.AP(yst_, base + 64 * lo,
                                            [[n_tt * 128, 128], [64, nsl], [1, 64]])
                                i0 = bass.AP(ps_y[b_], 0,
                                             [[512, 128], [65, nsl], [1, 64]])
                                i1 = bass.AP(rwave, 0, [[8, 128], [1, nsl], [0, 64]])
                                return v.tensor_tensor(o, i0, i1, ALU.mult)
                            vdv = S.emit("dv", tt_,
                                         waits=[("dv", vr1), G["yst%d" % par],
                                                ("q_w2", v_w2done),
                                                ("q_w3", v_w3done)],
                                         inc="dv")
                            G["ps_y%d" % b_] = ("dv", vdv)
                            G["rwave"] = ("dv", vdv)
                    kslot[0] += 1
                    if job_i < len(pend):
                        consume(pend[job_i](), vq_next)
                        job_i += 1

                for kk in range(kk_hi):
                    emit_scores(kk)
                    if kk > 0:
                        emit_pv(kk - 1)
                emit_pv(kk_hi - 1)
            while job_i < len(pend):
                consume(pend[job_i](), vq_next)
                job_i += 1
            G["qk%d" % par] = ("pe", S.val("pe"))
            if s_i == 0:
                emit_wbig_chunk(hp)
            v_evacs = S.val("dv")
            def yst_store(s, s_i=s_i, hp=hp, yst_=ystage[par]):
                src = bass.AP(yst_, 0, [[n_tt * 128, 128], [128, n_tt], [1, 128]])
                dst = ystage_d[s_i, hp].rearrange("(tt p) d -> p tt d", p=128)
                return s.dma_start(out=dst, in_=src)
            if hp < 6:
                vys = S.emit("sy", yst_store, waits=[("dv", v_evacs)],
                             inc="q_ys%d" % par, amt=16)
                G["yst%d" % par] = ("q_ys%d" % par, vys)
                ys_val.append(vys)
            else:
                v_ev67[hp] = v_evacs
            if hp == 6:
                v_xf_free = S.val("pe")
                for hp2 in range(6):
                    S.emit("sy", lambda a, s_i=s_i, hp2=hp2, x_feat=x_feat:
                           a.dma_start(
                               out=x_feat[:, hp2 * T:(hp2 + 1) * T],
                               in_=ystage_d[s_i, hp2][:, :],
                               transpose=True),
                           waits=[("q_ys%d" % (hp2 % 2), ys_val[hp2]),
                                  ("pe", v_xf_free)],
                           inc="q_yf", amt=16)
        nbg = (n_tt + 7) // 8
        y67_banks = {6: [ps_y[0], ps_y[1]], 7: [ps_y[2], ps_g]}
        y67_names = {6: ["ps_y0", "ps_y1"], 7: ["ps_y2", "ps_g"]}
        v_y67 = 0
        for hp2 in (6, 7):
            par2 = hp2 % 2
            vtp = 0
            for gb in range(nbg):
                ntl = min(8, n_tt - gb * 8)
                bank = y67_banks[hp2][gb].bitcast(BF16)
                bname = y67_names[hp2][gb]
                for c in range(ntl):
                    def ytp(t, bank=bank, c=c, gb=gb, ntl=ntl,
                            yst_=ystage[par2]):
                        return t.matmul(
                            bank[:, c * 128:(c + 1) * 128],
                            yst_[:, (gb * 8 + c) * 128:
                                 (gb * 8 + c + 1) * 128],
                            idb_sb[:], is_transpose=True,
                            start=(c == 0), stop=(c == ntl - 1))
                    wl = []
                    if c == 0:
                        wl = [("dv", max(v_ev67[hp2], v_idb)), G[bname]]
                    vtp = S.emit("pe", ytp, waits=wl,
                                 inc="pe" if c == ntl - 1 else None)
                def yev(v, hp2=hp2, gb=gb, bank=bank, ntl=ntl, x_feat=x_feat):
                    return v.tensor_copy(
                        x_feat[:, hp2 * T + gb * 1024:
                               hp2 * T + gb * 1024 + ntl * 128],
                        bank[:, 0:ntl * 128])
                v_y67 = S.emit("dv", yev, waits=[("pe", vtp)], inc="dv")
                G[bname] = ("dv", v_y67)
            G["yst%d" % par2] = ("pe", vtp)
        G["v_all"] = ("pe", S.val("pe"))
        v_attn_pe = S.val("pe")
        st_c.close()
        st_b = ExitStack()
        x_tok = st_b.enter_context(nc.sbuf_tensor(f"x_tok_{s_i}", [128, n_tt * 1024], F32))
        x1stage = st_b.enter_context(nc.sbuf_tensor(f"x1stage_{s_i}", [128, 2 * 1024], BF16))
        yfeat = x_feat
        v_yf = S.val("q_yf")

        xt_q4 = []
        ntq = max(1, n_tt // 4)
        for q4 in range(n_tt // ntq):
            def xt(s, s_i=s_i, q4=q4, ntq=ntq, x_tok=x_tok):
                src = x_in[s_i].rearrange("(tt p) c -> p tt c", p=128)[
                    :, q4 * ntq:(q4 + 1) * ntq]
                dst = x_tok[:].rearrange("p (tt c) -> p tt c", c=1024)[
                    :, q4 * ntq:(q4 + 1) * ntq]
                return s.dma_start(out=dst, in_=src)
            S.emit("po", xt, waits=[G["x_tok"], ("pe", v_attn_pe),
                                    ("q_ys0", S.val("q_ys0")),
                                    ("q_ys1", S.val("q_ys1")),
                                    ("q_w2", v_w2done), ("q_w3", v_w3done),
                                    ("q_xt", S.val("q_xt"))],
                   inc="q_xt", amt=16)
            xt_q4.append(S.val("q_xt"))
        v_xt = S.val("q_xt")

        def linear_ln(win_sb, w_wait, feat_ready_fn, bias_bc, v_bias, x1_mode,
                      v_po_base=0, yfeat=yfeat, x_tok=x_tok):
            stageB = [None] * n_tt

            def stage_a(tt):
                feat_ready = feat_ready_fn(tt)
                reg = ps_s[tt % 2]
                vpe = 0
                for jh in range(2):
                    def mmb(t, tt=tt, jh=jh, reg=reg, bias_bc=bias_bc):
                        return t.matmul(
                            reg[:, jh * 512:(jh + 1) * 512],
                            ones128[:],
                            bias_bc[:, jh * 512:(jh + 1) * 512],
                            start=True, stop=False)
                    w_list = [("dv", max(v_bbc, v_ones128))]
                    if jh == 0:
                        w_list.append(G["ps_s%d" % (tt % 2)])
                    S.emit("pe", mmb, waits=w_list)
                for dc in range(CC):
                    for jh in range(2):
                        def mm(t, tt=tt, dc=dc, jh=jh, reg=reg, win_sb=win_sb,
                               yfeat=yfeat):
                            return t.matmul(
                                reg[:, jh * 512:(jh + 1) * 512],
                                yfeat[:, dc * T + tt * 128:dc * T + (tt + 1) * 128],
                                win_sb[:, dc * 1024 + jh * 512:dc * 1024 + (jh + 1) * 512],
                                start=False, stop=(dc == CC - 1))
                        w_list = [feat_ready] + list(w_wait)
                        r = S.emit("pe", mm, waits=w_list,
                                   inc="pe" if (dc == CC - 1 and jh == 1) else None)
                        if r:
                            vpe = r
                par = tt % 2
                sl = tt % NLS
                mv = mv_sb[:, sl * 4:sl * 4 + 4]
                stt = stats_sb[:, sl * 12:sl * 12 + 12]
                def t2(v, tt=tt, par=par, x_tok=x_tok, reg=reg):
                    xt_ = x_tok[:, tt * 1024:(tt + 1) * 1024]
                    return v.tensor_tensor(xt_, xt_, reg, ALU.add)
                v_t2 = S.emit("dv", t2, waits=[("q_xt", xt_q4[tt // ntq]),
                                               ("pe", vpe),
                                               ("po", v_po_base)],
                              inc="dv")
                G["ps_s%d" % par] = ("dv", v_t2)
                def st1(v, tt=tt, x_tok=x_tok, stt=stt):
                    return v.bn_stats(stt[:, 0:6],
                                      x_tok[:, tt * 1024:tt * 1024 + 512])
                S.emit("dv", st1, waits=[("dv", v_t2)], inc="dv")
                def st2(v, tt=tt, x_tok=x_tok, stt=stt):
                    return v.bn_stats(stt[:, 6:12],
                                      x_tok[:, tt * 1024 + 512:(tt + 1) * 1024])
                vs2 = S.emit("dv", st2, inc="dv")
                def ag(v, mv=mv, stt=stt):
                    return v.bn_aggr(mv[:, 0:2], stt[:])
                vag = S.emit("dv", ag,
                             waits=[("dv", vs2), ("ac", ln_guard[sl]["ac"]),
                                    ("dv", ln_guard[sl]["dv"]),
                                    ("po", ln_guard[sl].get("po", 0))], inc="dv")
                def l1(a, mv=mv):
                    return a.activation(mv[:, 2:3], mv[:, 1:2], AF.Ln,
                                        bias=consts[:, 1:2], scale=1.0)
                va1 = S.emit("ac", l1, waits=[("dv", vag)], inc="ac")
                def l2(a, mv=mv):
                    return a.activation(mv[:, 2:3], mv[:, 2:3], AF.Exp,
                                        bias=consts[:, 2:3], scale=-0.5)
                va2 = S.emit("ac", l2, waits=[("ac", va1)], inc="ac")

                def stage_b1(tt=tt, par=par, sl=sl, mv=mv, va2=va2):
                    def nm(v, mv=mv):
                        return v.tensor_scalar(mv[:, 3:4], mv[:, 0:1],
                                               mv[:, 2:3], -1.0, ALU.mult, ALU.mult)
                    vnm = S.emit("dv", nm, waits=[("ac", va2)], inc="dv")
                    if x1_mode:
                        def xb(a, tt=tt, par=par, x_tok=x_tok, mv=mv):
                            return a.activation(
                                x1stage[:, par * 1024:(par + 1) * 1024],
                                x_tok[:, tt * 1024:(tt + 1) * 1024],
                                AF.Identity, bias=mv[:, 3:4], scale=mv[:, 2:3])
                        vxb = S.emit("ac", xb,
                                     waits=[("dv", vnm), G["x1s%d" % par]],
                                     inc="ac")
                        x1xb[tt] = vxb
                        def xf32(g, tt=tt, x_tok=x_tok, mv=mv):
                            return g.tensor_scalar(x_tok[:, tt * 1024:(tt + 1) * 1024],
                                                   x_tok[:, tt * 1024:(tt + 1) * 1024],
                                                   mv[:, 2:3], mv[:, 3:4],
                                                   ALU.mult, ALU.add)
                        vz = S.emit("po", xf32, waits=[("ac", vxb)], inc="po")
                        ln_guard[sl] = {"ac": vxb, "dv": 0, "po": vz}
                    else:
                        def xf32(g, tt=tt, x_tok=x_tok, mv=mv):
                            return g.tensor_scalar(x_tok[:, tt * 1024:(tt + 1) * 1024],
                                                   x_tok[:, tt * 1024:(tt + 1) * 1024],
                                                   mv[:, 2:3], mv[:, 3:4],
                                                   ALU.mult, ALU.add)
                        vz = S.emit("po", xf32, waits=[("dv", vnm)], inc="po")
                        def so(s, s_i=s_i, tt=tt, x_tok=x_tok):
                            return s.dma_start(out=out[s_i, tt * 128:(tt + 1) * 128, :],
                                               in_=x_tok[:, tt * 1024:(tt + 1) * 1024])
                        S.emit("sy", so, waits=[("po", vz)], inc="q_out", amt=16)
                        ln_guard[sl] = {"ac": va2, "dv": 0, "po": vz}
                return stage_b1

            def stage_b2(tt):
                par = tt % 2
                bank = tbank[tt % 4].bitcast(BF16)
                for c in range(CC):
                    def x1tp(t, bank=bank, c=c, par=par):
                        return t.matmul(
                            bank[:, c * 128:(c + 1) * 128],
                            x1stage[:, par * 1024 + c * 128:
                                    par * 1024 + (c + 1) * 128],
                            idb_sb[:], is_transpose=True,
                            start=(c == 0), stop=(c == CC - 1))
                    wl = []
                    if c == 0:
                        wl = [("ac", x1xb[tt]), ("dv", v_idb)]
                        if tt >= 4:
                            wl.append(("ac", x1evac[tt - 4]))
                        else:
                            b = tt % 4
                            wl.append(G["ps_y%d" % b] if b < 3 else G["ps_g"])
                    vtp = S.emit("pe", x1tp, waits=wl,
                                 inc="pe" if c == CC - 1 else None)
                G["x1s%d" % par] = ("pe", vtp)
                def x1ev(a, tt=tt, bank=bank, yfeat=yfeat):
                    o = bass.AP(yfeat, tt * 128,
                                [[CC * T, 128], [T, 8], [1, 128]])
                    return a.copy(o, bank[:, 0:1024])
                x1evac[tt] = S.emit("ac", x1ev, waits=[("pe", vtp)], inc="ac")

            for tt in range(n_tt):
                if x1_mode and tt > 2:
                    stage_b2(tt - 3)
                if tt > 0:
                    stageB[tt - 1]()
                stageB[tt] = stage_a(tt)
            if x1_mode and n_tt >= 3:
                stage_b2(n_tt - 3)
            stageB[n_tt - 1]()
            if x1_mode:
                for tt in range(max(0, n_tt - 2), n_tt):
                    stage_b2(tt)

        x1evac = {}
        x1xb = {}
        linear_ln(Wp_sb, [("q_w", v_w), ("dv", v_y67)],
                  lambda tt: ("q_yf", v_yf),
                  bp_bc, 0, True)
        for b in range(4):
            last = max(t for t in x1evac if t % 4 == b)
            gk = "ps_y%d" % b if b < 3 else "ps_g"
            G[gk] = ("ac", x1evac[last])
        linear_ln(Wbig_sb, v_wbig_holder["v"],
                  lambda tt: ("ac", x1evac[tt]),
                  bf_bc, 0, False, v_po_base=S.val("po"))
        G["yfeat"] = ("pe", S.val("pe"))
        G["x_feat"] = ("pe", S.val("pe"))
        G["x_tok"] = ("q_out", S.val("q_out"))
        st_b.close()
        st_seq.close()

    v_out_final = S.val("q_out")

    with st:
        with nc.Block() as block:
            @block.tensor
            def _(t):
                S.run("pe", t, sems)

            @block.scalar
            def _(a):
                S.run("ac", a, sems)

            @block.vector
            def _(v):
                S.run("dv", v, sems)

            @block.gpsimd
            def _(g):
                S.run("po", g, sems)

            @block.sync
            def _(s):
                S.run("sy", s, sems)
                s.wait_ge(sems["q_out"], v_out_final)

    return nc


_NC_CACHE = {}


def kernel(**inputs):
    x = np.asarray(inputs["L0_attn_x"], dtype=np.float32)
    w_attn = np.asarray(inputs["L0_attn_W_attn"], dtype=np.float32)
    b_attn = np.asarray(inputs["L0_attn_b_attn"], dtype=np.float32)
    w_proj = np.asarray(inputs["L0_attn_W_proj"], dtype=np.float32)
    b_proj = np.asarray(inputs["L0_attn_b_proj"], dtype=np.float32)
    g1r = np.asarray(inputs["L0_ffn_G1_R"], dtype=np.float32)
    g2r = np.asarray(inputs["L0_ffn_G2_R"], dtype=np.float32)
    g1i = np.asarray(inputs["L0_ffn_G1_I"], dtype=np.float32)
    g2i = np.asarray(inputs["L0_ffn_G2_I"], dtype=np.float32)
    b_real = np.asarray(inputs["L0_ffn_bias_real"], dtype=np.float32)
    b_imag = np.asarray(inputs["L0_ffn_bias_imag"], dtype=np.float32)

    mask = np.triu(np.ones((128, 128), dtype=np.float32)).astype(bfdt)
    eye = np.eye(128, dtype=np.float32)

    if "nc" not in _NC_CACHE:
        _NC_CACHE["nc"] = build(2, 16)
    nc = _NC_CACHE["nc"]

    in_maps = []
    for core in range(8):
        in_maps.append({
            "x_in": np.ascontiguousarray(x[core * 2:(core + 1) * 2]),
            "w_attn": w_attn, "b_attn": b_attn,
            "w_proj": w_proj, "b_proj": b_proj,
            "g1r": g1r, "g2r": g2r, "g1i": g1i, "g2i": g2i,
            "b_real": b_real, "b_imag": b_imag,
            "mask01": mask, "ident": eye,
        })
    res = run_bass_kernel_spmd(nc, in_maps, core_ids=list(range(8)))
    return np.concatenate([r["out"] for r in res.results], axis=0)
